# version 17
# baseline (speedup 1.0000x reference)
import contextlib
import numpy as np

import concourse.bass as bass
import concourse.mybir as mybir
from concourse.bass_utils import run_bass_kernel_spmd
from concourse.bass_core_rust import IndirectOffsetOnAxis

F32 = mybir.dt.float32
BF16 = mybir.dt.bfloat16
AF = mybir.ActivationFunctionType
ALU = mybir.AluOpType
AX = mybir.AxisListType
I32 = mybir.dt.int32

D = 1024
NH = 16
HD = 64
NP = 8
NE = 14
FF = 2048
SCALE = 1.0 / 32.0
ESHIFT = -20.0
EPS = 1e-5

ENGINES = ("tensor", "scalar", "vector", "gpsimd", "sync")


class Plan:

    def __init__(self, nc):
        self.nc = nc
        self.ops = {e: [] for e in ENGINES}
        self.cnt = {}
        self.esem = {}
        for e in ENGINES:
            s = nc.alloc_semaphore(f"sem_{e}")
            self.esem[e] = s
            self.cnt[s] = 0
        self.last_w = {}
        self.readers = {}
        self.floors = {e: {} for e in ENGINES}
        self.dma_sems = set()

    def sem(self, name):
        s = self.nc.alloc_semaphore(name)
        self.cnt[s] = 0
        self.dma_sems.add(s)
        return s

    def _wait(self, eng, sem, val):
        if sem in self.dma_sems:
            val = self.cnt[sem]
        if val <= 0:
            return
        fl = self.floors[eng]
        if fl.get(sem, 0) >= val:
            return
        fl[sem] = val
        self.ops[eng].append(lambda E, s=sem, v=val: E.wait_ge(s, v))

    def barrier(self, eng, keys):
        for k in keys:
            w = self.last_w.get(k)
            if w:
                self._wait(eng, *w)
            for s, c in self.readers.get(k, {}).items():
                self._wait(eng, s, c)

    def op(self, eng, build, reads=(), writes=(), dma_sem=None):
        for k in reads:
            w = self.last_w.get(k)
            if w:
                self._wait(eng, *w)
        for k in writes:
            w = self.last_w.get(k)
            if w:
                self._wait(eng, *w)
            for s, c in self.readers.get(k, {}).items():
                self._wait(eng, s, c)
        if dma_sem is not None:
            sem, n = dma_sem, 16
        else:
            sem, n = self.esem[eng], 1
        tgt = self.cnt[sem] + n
        self.cnt[sem] = tgt

        def fn(E, b=build, s=sem, v=n):
            b(E).then_inc(s, v)

        self.ops[eng].append(fn)
        for k in writes:
            self.last_w[k] = (sem, tgt)
            self.readers[k] = {}
        for k in reads:
            self.readers.setdefault(k, {})[sem] = tgt
        return (sem, tgt)


DBG = False


def build_nc(T):
    NTT = T // 128
    NQC = T // 256
    NKT = T // 128
    NTTH = NTT // 2
    NTC = T // 512

    nc = bass.Bass("TRN2", target_bir_lowering=False)

    d_x = nc.dram_tensor("x", [T, D], F32, kind="ExternalInput").ap()
    d_wqT = nc.dram_tensor("wqT", [D, D], F32, kind="ExternalInput").ap()
    d_wkT = nc.dram_tensor("wkT", [D, D], F32, kind="ExternalInput").ap()
    d_wvT = nc.dram_tensor("wvT", [D, D], F32, kind="ExternalInput").ap()
    d_woT = nc.dram_tensor("woT", [D, D], F32, kind="ExternalInput").ap()
    d_bqT = nc.dram_tensor("bqT", [128, 8], F32, kind="ExternalInput").ap()
    d_bkT = nc.dram_tensor("bkT", [128, 8], F32, kind="ExternalInput").ap()
    d_bv = nc.dram_tensor("bv", [1, D], F32, kind="ExternalInput").ap()
    d_bo = nc.dram_tensor("bo", [1, D], F32, kind="ExternalInput").ap()
    d_gwT = nc.dram_tensor("gwT", [D, NE], F32, kind="ExternalInput").ap()
    d_w1 = nc.dram_tensor("w1", [NE, D, FF], F32, kind="ExternalInput").ap()
    d_b1T = nc.dram_tensor("b1T", [NE, 128, 16], F32, kind="ExternalInput").ap()
    d_w2 = nc.dram_tensor("w2", [NE, FF, D], F32, kind="ExternalInput").ap()
    d_b2 = nc.dram_tensor("b2", [NE, D], F32, kind="ExternalInput").ap()
    d_ident = nc.dram_tensor("ident", [128, 128], F32, kind="ExternalInput").ap()
    d_triu1 = nc.dram_tensor("triu1", [128, 128], F32, kind="ExternalInput").ap()
    d_ebase = nc.dram_tensor("ebase", [1, NE], F32, kind="ExternalInput").ap()
    d_bound = nc.dram_tensor("boundv", [128, NE], F32,
                             kind="ExternalInput").ap()
    d_iotok = nc.dram_tensor("iotok", [128, T // 128], F32,
                             kind="ExternalInput").ap()
    d_tbli = nc.dram_tensor("tbli", [NE * (T // 2) + 128, 4], F32,
                            kind="ExternalInput").ap()
    d_tri = nc.dram_tensor("tri", [128, 128], F32, kind="ExternalInput").ap()
    d_ones = nc.dram_tensor("onesrow", [1, 128], F32, kind="ExternalInput").ap()
    d_out = nc.dram_tensor("out", [T, D], F32, kind="ExternalOutput").ap()
    CAP = T // 2
    NIC = NE * CAP // 128
    d_x1d = nc.dram_tensor("x1d", [T + 128, D], BF16, kind="Internal").ap()
    d_accB = nc.dram_tensor("accB", [T, D], F32, kind="Internal").ap()
    d_accd = nc.dram_tensor("accd", [2 * (T + 128), D], BF16,
                            kind="Internal").ap()
    d_tbl = nc.dram_tensor("tbl", [NE * CAP + 128, 4], F32,
                           kind="Internal").ap()

    p = Plan(nc)
    s_init = p.sem("s_init")
    s_init2 = p.sem("s_init2")
    s_xbf = [p.sem("s_xbf0"), p.sem("s_xbf1")]
    s_wsl = [p.sem("s_wq"), p.sem("s_wk"), p.sem("s_wv")]
    s_wo = p.sem("s_wo")
    s_xres = [p.sem("s_xres0"), p.sem("s_xres1")]
    s_w1 = p.sem("s_w1")
    s_w2 = p.sem("s_w2")
    s_bb = [p.sem("s_bb0"), p.sem("s_bb1")]
    s_out = [p.sem("s_out0"), p.sem("s_out1")]
    s_binit = p.sem("s_binit")
    s_tblw = p.sem("s_tblw")
    s_gx = [p.sem("s_gx0"), p.sem("s_gx1")]
    s_sc = p.sem("s_sc")
    s_acc = [p.sem("s_acc0"), p.sem("s_acc1")]
    s_x1l = [p.sem("s_x1l0"), p.sem("s_x1l1")]
    s_y1l = [p.sem("s_y1l0"), p.sem("s_y1l1")]
    s_y2l = [p.sem("s_y2l0"), p.sem("s_y2l1")]

    es_prog = contextlib.ExitStack()
    alloc = lambda es, *a, **kw: es.enter_context(nc.sbuf_tensor(*a, **kw))

    ident_bf = alloc(es_prog, "ident_bf", [128, 128], BF16)
    tri_bf = alloc(es_prog, "tri_bf", [128, 128], BF16)
    ones_bf = alloc(es_prog, "ones_bf", [1, 128], BF16)
    bqT_sb = alloc(es_prog, "bqT_sb", [128, 8], F32)
    bkT_sb = alloc(es_prog, "bkT_sb", [128, 8], F32)
    maskw = alloc(es_prog, "maskw", [128, NTT, NE], F32)


    onorm = [alloc(es_prog, f"onorm{i}", [128, 64], BF16) for i in range(8)]
    rd8 = alloc(es_prog, "rd8", [128, 8], F32)
    sc1 = {k: alloc(es_prog, f"sc_{k}", [128, 1], F32)
           for k in ("m1", "m2", "w0", "w1c", "rd", "ee")}
    scL = [{k: alloc(es_prog, f"scL_{k}_{i}", [128, 1], F32)
            for k in ("sums", "sumsq", "mean", "em2", "var", "rstd", "nmr")}
           for i in range(2)]
    sq_f32 = alloc(es_prog, "sq_f32", [128, D], F32)
    scE = {k: alloc(es_prog, f"scE_{k}", [128, NE], F32)
           for k in ("lg", "eq1", "eq2", "tmp")}
    cst = alloc(es_prog, "cst", [128, 4], F32)
    c_scale = cst[:, 0:1]
    c_eshift = cst[:, 1:2]
    c_mhalf = cst[:, 2:3]
    c_eps = cst[:, 3:4]

    es_act = contextlib.ExitStack()
    actT = alloc(es_act, "actT", [128, 8, T], BF16)

    es_ps_att = contextlib.ExitStack()
    pal = lambda es, *a, **kw: es.enter_context(nc.psum_tensor(*a, **kw))
    ps_st0 = pal(es_ps_att, "ps_st0", [128, 8, 128], F32)
    ps_st1 = pal(es_ps_att, "ps_st1", [128, 8, 128], F32)
    ps_o = [pal(es_ps_att, f"ps_o{i}", [128, 65], F32) for i in range(2)]
    ps_tr = [pal(es_ps_att, f"ps_tr{i}", [128, 128], BF16) for i in range(2)]
    _st_flat = [ps_st0[:].rearrange("p a b -> p (a b)"),
                ps_st1[:].rearrange("p a b -> p (a b)")]
    ps_proj = [_st_flat[0][:, 0:512], _st_flat[0][:, 512:1024],
               _st_flat[1][:, 0:512], _st_flat[1][:, 512:1024]]
    ps_proj_key = [("pstb", 0), ("pstb", 1), ("pstb", 2), ("pstb", 3)]

    es_b1 = contextlib.ExitStack()
    concatT = alloc(es_b1, "concatT", [128, 8, T], BF16)

    es_b2 = contextlib.ExitStack()
    QTp = alloc(es_b2, "QTp", [128, T], BF16)
    KTp = alloc(es_b2, "KTp", [128, T], BF16)
    Vp = alloc(es_b2, "Vp", [128, NTT, 2, 66], BF16)
    wq_sl = alloc(es_b2, "wq_sl", [128, 8, 128], BF16)
    wk_sl = alloc(es_b2, "wk_sl", [128, 8, 128], BF16)
    wv_sl = alloc(es_b2, "wv_sl", [128, 8, 128], BF16)
    bv_row = alloc(es_b2, "bv_row", [1, D], BF16)
    es_b2x = contextlib.ExitStack()
    xbf = [alloc(es_b2x, f"xbf{i}", [128, D], BF16) for i in range(2)]
    es_b2x.close()
    A0 = [alloc(es_b2, f"A0_{i}", [128, NKT, 128], BF16) for i in range(2)]
    A1 = [alloc(es_b2, f"A1_{i}", [128, NKT, 128], BF16) for i in range(2)]

    def dma(eng, out_ap, in_ap, sem, reads=(), writes=()):
        p.op(eng, lambda E, o=out_ap, i=in_ap: E.dma_start(o, i),
             reads=reads, writes=writes, dma_sem=sem)

    dma("gpsimd", ident_bf[:], d_ident, s_init, writes=[("ident",)])
    dma("gpsimd", tri_bf[:], d_tri, s_init, writes=[("tri",)])
    dma("gpsimd", ones_bf[:], d_ones, s_init, writes=[("ones",)])
    dma("sync", bqT_sb[:], d_bqT, s_init2, writes=[("bqT",)])
    dma("sync", bkT_sb[:], d_bkT, s_init2, writes=[("bkT",)])
    dma("gpsimd", bv_row[:], d_bv, s_init, writes=[("bv",)])
    for w_sl, d_w, si, nm in ((wq_sl, d_wqT, 0, "wq"), (wk_sl, d_wkT, 1, "wk"),
                              (wv_sl, d_wvT, 2, "wv")):
        dma("gpsimd", w_sl[:],
            d_w[:, 0:128].rearrange("(c q) m -> q c m", q=128),
            s_wsl[si], writes=[(nm,)])
    p.op("vector", lambda E: E.memset(Vp[:, :, :, 64:65], 1.0),
         writes=[("Vpones",)])
    dma("gpsimd", d_tbl, d_tbli, s_init, writes=[("tbl",)])
    for i, v in enumerate((SCALE, ESHIFT, -0.5, EPS)):
        p.op("vector", lambda E, i=i, v=v: E.memset(cst[:, i:i + 1], v),
             writes=[("cst",)])

    for tt in range(NTT):
        par = tt % 2
        dma("gpsimd", xbf[par][:], d_x[tt * 128:(tt + 1) * 128, :],
            s_xbf[par], writes=[("xbf", par)])
        for c in range(8):
            p.op("tensor",
                 lambda E, par=par, c=c: E.transpose(
                     ps_tr[c % 2][:], xbf[par][:, c * 128:(c + 1) * 128],
                     ident_bf[:]),
                 reads=[("xbf", par), ("ident",)], writes=[("ps_tr", c % 2)])
            p.op("vector",
                 lambda E, par=par, c=c, tt=tt: E.tensor_copy(
                     actT[:, c, tt * 128:(tt + 1) * 128], ps_tr[c % 2][:]),
                 reads=[("ps_tr", c % 2)], writes=[("actT", c, tt)])

    pp = 0
    po = [0]
    pt = [0]
    for pair in range(NP):
        for (w_sl, b_sb, dst, wkey, bkey, dkey) in (
                (wq_sl, bqT_sb, QTp, "wq", "bqT", "QTp"),
                (wk_sl, bkT_sb, KTp, "wk", "bkT", "KTp")):
            for tc in range(NTC):
                ps = ps_proj[pp % 4]
                pskey = ps_proj_key[pp % 4]
                pp += 1

                def mmg(E, w_sl=w_sl, tc=tc, ps=ps):
                    for c in range(8):
                        inst = E.matmul(ps, w_sl[:, c, :],
                                        actT[:, c, tc * 512:(tc + 1) * 512],
                                        start=(c == 0), stop=(c == 7))
                    return inst
                p.op("tensor", mmg,
                     reads=[(wkey,)] + [("actT", c, 4 * tc + j)
                                        for c in range(8) for j in range(4)],
                     writes=[pskey])
                p.op("vector",
                     lambda E, dst=dst, tc=tc, ps=ps, b_sb=b_sb, pair=pair:
                     E.tensor_scalar(dst[:, tc * 512:(tc + 1) * 512], ps,
                                     b_sb[:, pair:pair + 1], None, ALU.add),
                     reads=[pskey, (bkey,)], writes=[(dkey, tc)])
        for tt in range(NTT):
            ps = ps_proj[pp % 4]
            pskey = ps_proj_key[pp % 4]
            pp += 1

            def mmv(E, tt=tt, ps=ps, pair=pair):
                E.matmul(ps[:, 0:128], ones_bf[:],
                         bv_row[0:1, pair * 128:(pair + 1) * 128],
                         start=True, stop=False)
                for c in range(8):
                    inst = E.matmul(ps[:, 0:128],
                                    actT[:, c, tt * 128:(tt + 1) * 128],
                                    wv_sl[:, c, :],
                                    start=False, stop=(c == 7))
                return inst
            p.op("tensor", mmv,
                 reads=[("wv",), ("ones",), ("bv",)] +
                       [("actT", c, tt) for c in range(8)],
                 writes=[pskey])
            p.op("vector",
                 lambda E, tt=tt, ps=ps: E.tensor_copy(
                     Vp[:, tt, 0:2, 0:64],
                     ps[:, 0:128].rearrange("q (h f) -> q h f", h=2)),
                 reads=[pskey], writes=[("Vp", tt)])

        if pair + 1 < NP:
            for w_sl, d_w, si, nm in ((wq_sl, d_wqT, 0, "wq"),
                                      (wk_sl, d_wkT, 1, "wk"),
                                      (wv_sl, d_wvT, 2, "wv")):
                dma("gpsimd", w_sl[:],
                    d_w[:, (pair + 1) * 128:(pair + 2) * 128]
                    .rearrange("(c q) m -> q c m", q=128),
                    s_wsl[si], writes=[(nm,)])


        def st_span_ops(qt):
            apar = qt % 2
            spans = [(8 * s, min(8, qt + 1 - 8 * s))
                     for s in range((qt + 1 + 7) // 8)]

            def one(h, k0, nk, qt=qt, apar=apar):
                A_h, ps_h, pk0, p0, tp = (
                    (A0, ps_st0, 0, 0, (0, 0)),
                    (A1, ps_st1, 2, 64, (64, 0)))[h]

                def mmst(E, ps_h=ps_h, p0=p0, tp=tp, qt=qt, k0=k0, nk=nk):
                    for j in range(nk):
                        kt = k0 + j
                        inst = E.matmul(
                            ps_h[:, j, :],
                            KTp[p0:p0 + 64, kt * 128:(kt + 1) * 128],
                            QTp[p0:p0 + 64, qt * 128:(qt + 1) * 128],
                            start=True, stop=True, tile_position=tp)
                    return inst
                p.op("tensor", mmst,
                     reads=[("QTp", qt // 4), ("KTp", k0 // 4),
                            ("KTp", (k0 + nk - 1) // 4)],
                     writes=[("pstb", pk0), ("pstb", pk0 + 1)])
                p.op("scalar",
                     lambda E, A_h=A_h, ps_h=ps_h, k0=k0, nk=nk, apar=apar:
                     E.activation(A_h[apar][:, k0:k0 + nk, :],
                                  ps_h[:, 0:nk, :], AF.Exp,
                                  bias=c_eshift, scale=c_scale),
                     reads=[("pstb", pk0), ("pstb", pk0 + 1), ("cst",)],
                     writes=[("A", h, apar, k0 + j) for j in range(nk)])

            ops = []
            for i in range(len(spans) + 1):
                def emit(i=i, spans=spans):
                    if i > 0:
                        one(1, *spans[i - 1])
                    if i < len(spans):
                        one(0, *spans[i])
                ops.append(emit)

            def emit_mask(qt=qt, apar=apar):
                for h, A_h in enumerate((A0, A1)):
                    p.op("vector",
                         lambda E, A_h=A_h, apar=apar, qt=qt: E.tensor_tensor(
                             A_h[apar][:, qt, :], A_h[apar][:, qt, :],
                             tri_bf[:], ALU.mult),
                         reads=[("A", h, apar, qt), ("tri",)],
                         writes=[("A", h, apar, qt)])
            ops.append(emit_mask)
            return ops

        def av_ops(qt):
            apar = qt % 2
            ops = []
            fini = []
            for h, A_h in enumerate((A0, A1)):
                opar = po[0] % 2
                po[0] += 1
                slot = (2 * qt + h) % 8
                nmm = qt + 1
                nch = max(1, (nmm + 7) // 8)
                bounds = [(i * nmm // nch, (i + 1) * nmm // nch)
                          for i in range(nch)]
                for ci, (a, bnd) in enumerate(bounds):
                    def emit(A_h=A_h, h=h, qt=qt, apar=apar, opar=opar,
                             a=a, bnd=bnd):
                        def mmav(E):
                            for kt in range(a, bnd):
                                inst = E.matmul(ps_o[opar][:],
                                                A_h[apar][:, kt, :],
                                                Vp[:, kt, h, 0:65],
                                                start=(kt == 0),
                                                stop=(kt == qt))
                            return inst
                        p.op("tensor", mmav,
                             reads=[("A", h, apar, kt)
                                    for kt in range(a, bnd)] +
                                   [("Vp", kt) for kt in range(a, bnd)] +
                                   [("Vpones",)],
                             writes=[("ps_o", opar)])
                    ops.append(emit)

                def emit_norm(h=h, qt=qt, opar=opar, slot=slot):
                    p.op("vector",
                         lambda E, opar=opar, slot=slot: E.reciprocal(
                             rd8[:, slot:slot + 1], ps_o[opar][:, 64:65]),
                         reads=[("ps_o", opar)], writes=[("rd", slot)])
                    p.op("vector",
                         lambda E, opar=opar, slot=slot: E.tensor_scalar(
                             onorm[slot][:], ps_o[opar][:, 0:64],
                             rd8[:, slot:slot + 1], None, ALU.mult),
                         reads=[("ps_o", opar), ("rd", slot)],
                         writes=[("onorm", slot)])
                ops.append(emit_norm)

                def emit_tr(h=h, qt=qt, slot=slot):
                    tpar = pt[0] % 2
                    pt[0] += 1
                    p.op("tensor",
                         lambda E, slot=slot, tpar=tpar: E.transpose(
                             ps_tr[tpar][0:64, :], onorm[slot][:],
                             ident_bf[:]),
                         reads=[("onorm", slot), ("ident",)],
                         writes=[("ps_tr", tpar)])
                    p.op("vector",
                         lambda E, h=h, qt=qt, tpar=tpar, pair=pair:
                         E.tensor_copy(concatT[h * 64:(h + 1) * 64, pair,
                                               qt * 128:(qt + 1) * 128],
                                       ps_tr[tpar][0:64, :]),
                         reads=[("ps_tr", tpar)],
                         writes=[("cat", pair, qt)])
                fini.append(emit_tr)
            return ops, fini

        prev_av = []
        pending_tr = []
        for qt in range(NKT):
            st = st_span_ops(qt)
            n = max(len(st), len(prev_av))
            for i in range(n):
                if i < len(st):
                    st[i]()
                if i < len(prev_av):
                    prev_av[i]()
            while len(pending_tr) > 6:
                pending_tr.pop(0)()
            prev_av, fin = av_ops(qt)
            pending_tr.extend(fin)
        for e in prev_av:
            e()
        for f in pending_tr:
            f()

    attdead = ([("QTp", i) for i in range(NTC)] +
               [("KTp", i) for i in range(NTC)] +
               [("Vp", tt) for tt in range(NTT)] + [("Vpones",)] +
               [("A", h, kt) for h in range(2) for kt in range(NKT)] +
               [("wq",), ("wk",), ("wv",), ("bv",),
                ("xbf", 0), ("xbf", 1)])
    es_b2.close()
    es_c = contextlib.ExitStack()
    woT_sb = alloc(es_c, "woT_sb", [128, 8, D], BF16)
    bo_row = alloc(es_c, "bo_row", [1, D], BF16)
    xres = [alloc(es_c, f"xres{i}", [128, D], F32) for i in range(2)]
    t_f32 = [alloc(es_c, f"t_f32{i}", [128, D], F32) for i in range(2)]
    x1n = [alloc(es_c, f"x1n{i}", [128, D], BF16) for i in range(2)]
    gwT_sb = alloc(es_c, "gwT_sb", [128, 8, NE], BF16)
    b2_all = alloc(es_c, "b2_all", [14, D], BF16)
    mwT = alloc(es_c, "mwT", [14, NTT, 128], BF16)
    mwb16 = alloc(es_c, "mwb16", [128, NE], BF16)
    iotok_sb = alloc(es_c, "iotok_sb", [128, T // 128], F32)
    triu1_bf = alloc(es_c, "triu1_bf", [128, 128], BF16)
    onescol = alloc(es_c, "onescol", [128, 1], BF16)
    ones_f32 = alloc(es_c, "ones_f32", [1, 128], F32)
    base_row = alloc(es_c, "base_row", [1, NE], F32)
    m_bf = alloc(es_c, "m_bf", [128, NE], BF16)
    bsb = alloc(es_c, "bsb", [128, NE], F32)
    sf3 = alloc(es_c, "sf3", [128, NE], F32)
    boundc = alloc(es_c, "boundc", [128, NE], F32)
    pr = [alloc(es_c, f"pr{i}", [128, 4], F32) for i in range(2)]
    fsi = [alloc(es_c, f"fsi{i}", [128, 1], I32) for i in range(2)]

    p.barrier("gpsimd", attdead)
    dma("gpsimd", woT_sb[:], d_woT.rearrange("(c q) m -> q c m", q=128),
        s_wo, writes=[("woT",)])
    dma("gpsimd", bo_row[:], d_bo, s_wo, writes=[("bo",)])
    dma("gpsimd", gwT_sb[:], d_gwT.rearrange("(c q) m -> q c m", q=128),
        s_wo, writes=[("gwT",)])
    dma("gpsimd", b2_all[:], d_b2, s_wo, writes=[("b2a",)])
    dma("gpsimd", triu1_bf[:], d_triu1, s_wo, writes=[("triu1",)])
    p.barrier("sync", attdead)
    dma("sync", base_row[:], d_ebase, s_init2, writes=[("base",)])
    dma("sync", boundc[:], d_bound, s_init2, writes=[("boundc",)])
    dma("sync", iotok_sb[:], d_iotok, s_init2, writes=[("iotok",)])
    p.op("vector", lambda E: E.memset(onescol[:], 1.0), writes=[("onescol",)])
    p.op("vector", lambda E: E.memset(ones_f32[:], 1.0), writes=[("onesf",)])
    for j in range(2):
        p.op("vector", lambda E, j=j: E.memset(pr[j][:, 3:4], 0.0),
             writes=[("prpad", j)])

    attps = ([("pstb", i) for i in range(4)] +
             [("ps_o", i) for i in range(2)] + [("ps_tr", i) for i in range(2)])
    es_ps_att.close()
    ps_out = [pal(es_prog, f"ps_out{i}", [128, D], F32) for i in range(2)]
    es_ps_3 = contextlib.ExitStack()
    ps_g = [pal(es_ps_3, f"ps_g{i}", [128, 128], F32) for i in range(2)]
    ps_t3 = [pal(es_ps_3, f"ps_t3{i}", [128, 128], BF16) for i in range(2)]
    p.barrier("tensor", attps)

    def ln_stats_and_apply(out_ap, out_key, par, tbuf, tkey):
        SC = scL[par]

        def k(name):
            return (name, par)
        p.op("vector",
             lambda E, tbuf=tbuf: E.reduce_sum(SC["sums"][:], tbuf[:],
                                               axis=AX.X),
             reads=[tkey], writes=[k("sums")])
        p.op("scalar",
             lambda E, tbuf=tbuf: E.activation(sq_f32[:], tbuf[:],
                                               AF.Square),
             reads=[tkey], writes=[("sq",)])
        p.op("vector",
             lambda E: E.reduce_sum(SC["sumsq"][:], sq_f32[:], axis=AX.X),
             reads=[("sq",)], writes=[k("sumsq")])
        p.op("vector",
             lambda E: E.tensor_scalar(SC["mean"][:], SC["sums"][:],
                                       1.0 / D, None, ALU.mult),
             reads=[k("sums")], writes=[k("mean")])
        p.op("vector",
             lambda E: E.tensor_scalar(SC["em2"][:], SC["sumsq"][:],
                                       1.0 / D, None, ALU.mult),
             reads=[k("sumsq")], writes=[k("em2")])
        p.op("vector",
             lambda E: E.tensor_tensor(SC["var"][:], SC["mean"][:],
                                       SC["mean"][:], ALU.mult),
             reads=[k("mean")], writes=[k("var")])
        p.op("vector",
             lambda E: E.tensor_sub(SC["var"][:], SC["em2"][:], SC["var"][:]),
             reads=[k("em2"), k("var")], writes=[k("var")])
        p.op("scalar",
             lambda E: E.activation(SC["rstd"][:], SC["var"][:], AF.Ln,
                                    bias=c_eps),
             reads=[k("var"), ("cst",)], writes=[k("rstd")])
        p.op("scalar",
             lambda E: E.activation(SC["rstd"][:], SC["rstd"][:], AF.Exp,
                                    scale=c_mhalf),
             reads=[k("rstd"), ("cst",)], writes=[k("rstd")])
        p.op("vector",
             lambda E: E.scalar_tensor_tensor(SC["nmr"][:], SC["mean"][:],
                                              -1.0, SC["rstd"][:],
                                              ALU.mult, ALU.mult),
             reads=[k("mean"), k("rstd")], writes=[k("nmr")])
        p.op("vector",
             lambda E, out_ap=out_ap, tbuf=tbuf: E.tensor_scalar(
                 out_ap, tbuf[:], SC["rstd"][:], SC["nmr"][:],
                 ALU.mult, ALU.add),
             reads=[tkey, k("rstd"), k("nmr")], writes=[out_key])

    dma("sync", xres[0][:], d_x[0:128, :], s_xres[0], writes=[("xres", 0)])
    for tt in range(NTT):
        par = tt % 2
        if tt + 1 < NTT:
            dma("sync", xres[(tt + 1) % 2][:],
                d_x[(tt + 1) * 128:(tt + 2) * 128, :],
                s_xres[(tt + 1) % 2], writes=[("xres", (tt + 1) % 2)])

        def mmo(E, tt=tt, par=par):
            E.matmul(ps_out[par][:, 0:512], ones_bf[:], bo_row[0:1, 0:512],
                     start=True, stop=False)
            E.matmul(ps_out[par][:, 512:1024], ones_bf[:],
                     bo_row[0:1, 512:1024], start=True, stop=False)
            for c in range(8):
                for n in range(2):
                    inst = E.matmul(ps_out[par][:, n * 512:(n + 1) * 512],
                                    concatT[:, c, tt * 128:(tt + 1) * 128],
                                    woT_sb[:, c, n * 512:(n + 1) * 512],
                                    start=False, stop=(c == 7))
            return inst
        p.op("tensor", mmo,
             reads=[("woT",), ("bo",), ("ones",)] +
                   [("cat", c, tt) for c in range(8)],
             writes=[("ps_out", par)])
        p.op("vector",
             lambda E, par=par: E.tensor_tensor(
                 t_f32[par][:], ps_out[par][:], xres[par][:], ALU.add),
             reads=[("ps_out", par), ("xres", par)], writes=[("t", par)])
        ln_stats_and_apply(x1n[par][:], ("x1n", par), par, t_f32[par],
                           ("t", par))
        dma("sync", d_x1d[tt * 128:(tt + 1) * 128, :], x1n[par][:],
            s_xres[par], reads=[("x1n", par)], writes=[("x1d", tt)])
        for c in range(8):
            p.op("tensor",
                 lambda E, par=par, c=c: E.transpose(
                     ps_t3[c % 2][:], x1n[par][:, c * 128:(c + 1) * 128],
                     ident_bf[:]),
                 reads=[("x1n", par), ("ident",)], writes=[("ps_t3", c % 2)])
            p.op("vector",
                 lambda E, c=c, tt=tt: E.tensor_copy(
                     actT[:, c, tt * 128:(tt + 1) * 128], ps_t3[c % 2][:]),
                 reads=[("ps_t3", c % 2)], writes=[("actT", c, tt)])


        def mmgate(E, tt=tt, par=par):
            for c in range(8):
                inst = E.matmul(ps_g[par][:, 0:NE],
                                actT[:, c, tt * 128:(tt + 1) * 128],
                                gwT_sb[:, c, :],
                                start=(c == 0), stop=(c == 7))
            return inst
        p.op("tensor", mmgate,
             reads=[("gwT",)] + [("actT", c, tt) for c in range(8)],
             writes=[("ps_g", par)])
        p.op("vector",
             lambda E, par=par: E.tensor_copy(scE["lg"][:], ps_g[par][:, 0:NE]),
             reads=[("ps_g", par)], writes=[("lg",)])
        p.op("vector",
             lambda E: E.reduce_max(sc1["m1"][:], scE["lg"][:], axis=AX.X),
             reads=[("lg",)], writes=[("m1",)])
        p.op("vector",
             lambda E: E.tensor_scalar(scE["eq1"][:], scE["lg"][:],
                                       sc1["m1"][:], None, ALU.is_ge),
             reads=[("lg",), ("m1",)], writes=[("eq1",)])
        p.op("vector",
             lambda E: E.scalar_tensor_tensor(scE["tmp"][:], scE["eq1"][:],
                                              -1e9, scE["lg"][:],
                                              ALU.mult, ALU.add),
             reads=[("eq1",), ("lg",)], writes=[("tmpE",)])
        p.op("vector",
             lambda E: E.reduce_max(sc1["m2"][:], scE["tmp"][:], axis=AX.X),
             reads=[("tmpE",)], writes=[("m2",)])
        p.op("vector",
             lambda E: E.tensor_scalar(scE["eq2"][:], scE["lg"][:],
                                       sc1["m2"][:], None, ALU.is_ge),
             reads=[("lg",), ("m2",)], writes=[("eq2",)])
        p.op("vector",
             lambda E: E.tensor_sub(sc1["ee"][:], sc1["m2"][:], sc1["m1"][:]),
             reads=[("m1",), ("m2",)], writes=[("ee",)])
        p.op("scalar",
             lambda E: E.activation(sc1["ee"][:], sc1["ee"][:], AF.Exp),
             reads=[("ee",)], writes=[("ee",)])
        p.op("vector",
             lambda E: E.tensor_scalar(sc1["ee"][:], sc1["ee"][:], 1.0, None,
                                       ALU.add),
             reads=[("ee",)], writes=[("ee",)])
        p.op("vector",
             lambda E: E.reciprocal(sc1["w0"][:], sc1["ee"][:]),
             reads=[("ee",)], writes=[("w0",)])
        p.op("vector",
             lambda E: E.tensor_scalar(sc1["w1c"][:], sc1["w0"][:], -1.0, 1.0,
                                       ALU.mult, ALU.add),
             reads=[("w0",)], writes=[("w1c",)])
        p.op("vector",
             lambda E: E.tensor_tensor(scE["tmp"][:], scE["eq2"][:],
                                       scE["eq1"][:], ALU.mult),
             reads=[("eq1",), ("eq2",)], writes=[("tmpE",)])
        p.op("vector",
             lambda E: E.tensor_sub(scE["tmp"][:], scE["eq2"][:],
                                    scE["tmp"][:]),
             reads=[("eq2",), ("tmpE",)], writes=[("tmpE",)])
        p.op("vector",
             lambda E: E.tensor_scalar(scE["tmp"][:], scE["tmp"][:],
                                       sc1["w1c"][:], None, ALU.mult),
             reads=[("tmpE",), ("w1c",)], writes=[("tmpE",)])
        p.op("vector",
             lambda E: E.tensor_scalar(scE["eq1"][:], scE["eq1"][:],
                                       sc1["w0"][:], None, ALU.mult),
             reads=[("eq1",), ("w0",)], writes=[("eq1",)])
        p.op("vector",
             lambda E, tt=tt: E.tensor_tensor(maskw[:, tt, :], scE["eq1"][:],
                                              scE["tmp"][:], ALU.add),
             reads=[("eq1",), ("tmpE",)], writes=[("mw", tt)])
        p.op("vector",
             lambda E, tt=tt: E.tensor_copy(mwb16[:], maskw[:, tt, :]),
             reads=[("mw", tt)], writes=[("mwb",)])
        p.op("tensor",
             lambda E, par=par: E.transpose(ps_t3[par][0:14, :], mwb16[:],
                                            ident_bf[:]),
             reads=[("mwb",), ("ident",)], writes=[("ps_t3", par)])
        p.op("vector",
             lambda E, tt=tt, par=par: E.tensor_copy(mwT[:, tt, :],
                                                     ps_t3[par][0:14, :]),
             reads=[("ps_t3", par)], writes=[("mwT", tt)])
        def mmb2(E, tt=tt, par=par):
            for n in range(2):
                inst = E.matmul(ps_out[par][:, n * 512:(n + 1) * 512],
                                mwT[:, tt, :],
                                b2_all[:, n * 512:(n + 1) * 512],
                                start=True, stop=True)
            return inst
        p.op("tensor", mmb2, reads=[("mwT", tt), ("b2a",)],
             writes=[("ps_out", par)])
        p.op("vector",
             lambda E, par=par: E.tensor_copy(t_f32[par][:], ps_out[par][:]),
             reads=[("ps_out", par)], writes=[("t", par)])
        dma("sync", d_accB[tt * 128:(tt + 1) * 128, :], t_f32[par][:],
            s_out[par], reads=[("t", par)], writes=[("accd_init", tt)])
        p.op("vector",
             lambda E, tt=tt: E.tensor_scalar(m_bf[:], maskw[:, tt, :], 0.0,
                                              None, ALU.is_gt),
             reads=[("mw", tt)], writes=[("memb",)])
        p.op("tensor",
             lambda E: E.matmul(ps_g[0][:, 0:NE], triu1_bf[:], m_bf[:],
                                start=True, stop=True),
             reads=[("triu1",), ("memb",)], writes=[("ps_g", 0)])
        p.op("tensor",
             lambda E: E.matmul(ps_g[1][:, 0:NE], ones_f32[:], base_row[:],
                                start=True, stop=True),
             reads=[("onesf",), ("base",)], writes=[("ps_g", 1)])
        p.op("vector",
             lambda E: E.tensor_copy(bsb[:], ps_g[1][:, 0:NE]),
             reads=[("ps_g", 1)], writes=[("bsb",)])
        p.op("tensor",
             lambda E: E.matmul(ps_g[1][0:1, 64:64 + NE], onescol[:],
                                m_bf[:], start=True, stop=True),
             reads=[("onescol",), ("memb",)], writes=[("ps_g", 1)])
        p.op("vector",
             lambda E: E.scalar_tensor_tensor(sf3[:], ps_g[0][:, 0:NE], 1.0,
                                              bsb[:], ALU.mult, ALU.add),
             reads=[("ps_g", 0), ("bsb",)], writes=[("sf",)])
        p.op("vector",
             lambda E: E.tensor_tensor(sf3[:], sf3[:], boundc[:], ALU.min),
             reads=[("sf",), ("boundc",)], writes=[("sf",)])
        p.op("vector",
             lambda E, tt=tt: E.tensor_scalar(scE["tmp"][:], maskw[:, tt, :],
                                              0.0, None, ALU.is_le),
             reads=[("mw", tt)], writes=[("tmpE",)])
        p.op("vector",
             lambda E: E.scalar_tensor_tensor(sf3[:], scE["tmp"][:], 1e6,
                                              sf3[:], ALU.mult, ALU.add),
             reads=[("tmpE",), ("sf",)], writes=[("sf",)])
        p.op("vector",
             lambda E: E.tensor_scalar(sf3[:], sf3[:], float(NE * CAP), None,
                                       ALU.min),
             reads=[("sf",)], writes=[("sf",)])
        p.op("vector",
             lambda E: E.tensor_tensor(base_row[:], base_row[:],
                                       ps_g[1][0:1, 64:64 + NE], ALU.add),
             reads=[("base",), ("ps_g", 1)], writes=[("base",)])
        p.op("vector",
             lambda E: E.tensor_reduce(sc1["m1"][:], sf3[:], axis=AX.X,
                                       op=ALU.min),
             reads=[("sf",)], writes=[("m1",)])
        p.op("vector",
             lambda E: E.tensor_scalar(scE["eq1"][:], sf3[:], sc1["m1"][:],
                                       None, ALU.is_le),
             reads=[("sf",), ("m1",)], writes=[("eq1",)])
        p.op("vector",
             lambda E: E.scalar_tensor_tensor(scE["tmp"][:], scE["eq1"][:],
                                              1e6, sf3[:], ALU.mult, ALU.add),
             reads=[("eq1",), ("sf",)], writes=[("tmpE",)])
        p.op("vector",
             lambda E: E.tensor_reduce(sc1["m2"][:], scE["tmp"][:],
                                       axis=AX.X, op=ALU.min),
             reads=[("tmpE",)], writes=[("m2",)])
        p.op("vector",
             lambda E: E.tensor_scalar(scE["eq2"][:], scE["tmp"][:],
                                       sc1["m2"][:], None, ALU.is_le),
             reads=[("tmpE",), ("m2",)], writes=[("eq2",)])
        for j, (fs, eqk) in enumerate((("m1", "eq1"), ("m2", "eq2"))):
            p.op("vector",
                 lambda E, tt=tt, eqk=eqk: E.tensor_tensor(
                     scE["lg"][:], maskw[:, tt, :], scE[eqk][:], ALU.mult),
                 reads=[("mw", tt), (eqk,)], writes=[("lg",)])
            p.op("vector",
                 lambda E, j=j: E.reduce_sum(pr[j][:, 1:2], scE["lg"][:],
                                             axis=AX.X),
                 reads=[("lg",)], writes=[("prw", j)])
            p.op("vector",
                 lambda E, j=j, tt=tt: E.tensor_copy(pr[j][:, 0:1],
                                                     iotok_sb[:, tt:tt + 1]),
                 reads=[("iotok",)], writes=[("prt", j)])
            p.op("vector",
                 lambda E, j=j, tt=tt: E.tensor_scalar(
                     pr[j][:, 2:3], iotok_sb[:, tt:tt + 1],
                     float(j * (T + 128)), None, ALU.add),
                 reads=[("iotok",)], writes=[("prd", j)])
            p.op("vector",
                 lambda E, j=j, fs=fs: E.tensor_copy(fsi[j][:],
                                                     sc1[fs][:]),
                 reads=[(fs,)], writes=[("fsi", j)])
            p.op("gpsimd",
                 lambda E, j=j: E.indirect_dma_start(
                     d_tbl, IndirectOffsetOnAxis(fsi[j][:], 0), pr[j][:],
                     None),
                 reads=[("prw", j), ("prt", j), ("prd", j), ("prpad", j),
                        ("fsi", j), ("tbl",)],
                 writes=[("tblw", tt, j)], dma_sem=s_tblw)

    if DBG:
        d_dbg_cat = nc.dram_tensor("dbg_cat", [128, 8, T], BF16,
                                   kind="ExternalOutput").ap()
        d_dbg_x1T = nc.dram_tensor("dbg_x1T", [128, 8, T], BF16,
                                   kind="ExternalOutput").ap()
        d_dbg_mw = nc.dram_tensor("dbg_mw", [128, NTT, NE], F32,
                                  kind="ExternalOutput").ap()
        s_dbg = p.sem("s_dbg")
        dma("sync", d_dbg_cat, concatT[:], s_dbg,
            reads=[("cat", c, tt) for c in range(8) for tt in range(NTT)])
        dma("sync", d_dbg_x1T, actT[:], s_dbg,
            reads=[("actT", c, tt) for c in range(8) for tt in range(NTT)])
        dma("sync", d_dbg_mw, maskw[:], s_dbg,
            reads=[("mw", tt) for tt in range(NTT)])
        p._wait("sync", s_dbg, p.cnt[s_dbg])

    NST = CAP // 128
    CH = min(1024, CAP)
    NCH = CAP // CH
    NSTC = CH // 128
    c_dead = [("woT",), ("bo",), ("xres", 0), ("xres", 1), ("gwT",),
              ("x1n", 0), ("x1n", 1), ("sq",), ("t", 0), ("t", 1),
              ("base",), ("boundc",), ("iotok",), ("bsb",), ("sf",),
              ("memb",), ("mwb",), ("b2a",), ("triu1",), ("onescol",),
              ("onesf",), ("prpad", 0), ("prpad", 1), ("prw", 0), ("prw", 1),
              ("prt", 0), ("prt", 1), ("prd", 0), ("prd", 1),
              ("fsi", 0), ("fsi", 1), ("lg",), ("eq1",), ("eq2",),
              ("tmpE",)] + \
             [("mwT", tt) for tt in range(NTT)] + \
             [("cat", c, tt) for c in range(8) for tt in range(NTT)] + \
             [("actT", c, tt) for c in range(8) for tt in range(NTT)]
    es_c.close()
    es_b1.close()
    es_act.close()
    w1_sb = alloc(es_prog, "w1_sb", [128, 8, FF], BF16)
    w2_sb = alloc(es_prog, "w2_sb", [128, 16, D], BF16)
    x1g = [alloc(es_prog, f"x1g{i}", [128, NSTC, D], BF16) for i in range(2)]
    x1gT = alloc(es_prog, "x1gT", [128, 8, CH], BF16)
    hh_sb = alloc(es_prog, "hh_sb", [128, 16, CH], BF16)
    yw = [alloc(es_prog, f"yw{i}", [128, D], BF16) for i in range(2)]
    tw_sb = alloc(es_prog, "tw_sb", [128, NIC, 4], F32)
    idx_i32 = alloc(es_prog, "idx_i32", [128, NIC], I32)
    idx2_i32 = alloc(es_prog, "idx2_i32", [128, NIC], I32)
    b1_sb = [alloc(es_prog, f"b1_sb{i}", [128, 16], F32) for i in range(2)]
    t2_f32 = [alloc(es_prog, f"t2_f32{i}", [128, D], F32) for i in range(2)]
    x1l = [alloc(es_prog, f"x1l{i}", [128, D], BF16) for i in range(2)]
    accl = [alloc(es_prog, f"accl{i}", [128, D], F32) for i in range(2)]
    y1l = [alloc(es_prog, f"y1l{i}", [128, D], BF16) for i in range(2)]
    y2l = [alloc(es_prog, f"y2l{i}", [128, D], BF16) for i in range(2)]
    ostg = [alloc(es_prog, f"ostg{i}", [128, D], F32) for i in range(2)]

    es_ps_3.close()
    ps_hh = [pal(es_prog, f"ps_hh{i}", [128, 512], F32) for i in range(2)]
    ps_y = ps_out
    ps_fi = [pal(es_prog, f"ps_fi{i}", [128, 128], BF16) for i in range(2)]
    p3ps = [("ps_g", 0), ("ps_g", 1), ("ps_t3", 0), ("ps_t3", 1)]
    p.barrier("tensor", p3ps)
    p.barrier("gpsimd", c_dead)
    p.barrier("sync", c_dead)
    p.barrier("vector", c_dead)
    p.barrier("scalar", c_dead)
    p.barrier("tensor", c_dead)

    p.op("vector", lambda E: E.memset(x1g[0][:, 0, :], 0.0),
         writes=[("x1g", 0, 0)])
    dma("sync", d_x1d[T:T + 128, :], x1g[0][:, 0, :], s_init2,
        reads=[("x1g", 0, 0)], writes=[("x1d", "trash")])
    dma("sync", tw_sb[:],
        d_tbl[0:NE * CAP, :].rearrange("(q r) c -> r q c", r=128),
        s_init2,
        reads=[("tblw", tt, j) for tt in range(NTT) for j in range(2)] +
              [("tbl",)],
        writes=[("tw",)])
    p.op("vector",
         lambda E: E.tensor_copy(idx_i32[:], tw_sb[:, :, 0]),
         reads=[("tw",)], writes=[("idx",)])
    p.op("vector",
         lambda E: E.tensor_copy(idx2_i32[:], tw_sb[:, :, 2]),
         reads=[("tw",)], writes=[("idx2",)])

    phh = 0
    pyy = 0
    chunks = [(e, ch) for e in range(NE) for ch in range(NCH)]

    def emit_w1(e):
        dma("gpsimd", b1_sb[e % 2][:], d_b1T[e], s_bb[e % 2],
            writes=[("b1", e % 2)])
        for half in range(2):
            dma("gpsimd", w1_sb[:, :, half * 1024:(half + 1) * 1024],
                d_w1[e].rearrange("(c q) f -> q c f", q=128)
                [:, :, half * 1024:(half + 1) * 1024],
                s_w1, writes=[("w1", half)])

    def emit_w2(e):
        for half in range(2):
            dma("gpsimd", w2_sb[:, half * 8:(half + 1) * 8, :],
                d_w2[e].rearrange("(c q) m -> q c m", q=128)
                [:, half * 8:(half + 1) * 8, :],
                s_w2, writes=[("w2", half)])

    def emit_gathers(ci):
        e, ch = chunks[ci]
        gpar = ci % 2
        for st in range(NSTC):
            icol = e * (CAP // 128) + ch * NSTC + st
            p.op("gpsimd",
                 lambda E, icol=icol, st=st, gpar=gpar: E.indirect_dma_start(
                     x1g[gpar][:, st, :], None, d_x1d,
                     IndirectOffsetOnAxis(idx_i32[:, icol:icol + 1], 0)),
                 reads=[("idx",), ("x1d", "trash")] +
                       [("x1d", tt) for tt in range(NTT)],
                 writes=[("x1g", gpar, st)], dma_sem=s_gx[gpar])

    emit_gathers(0)
    for ci, (e, ch) in enumerate(chunks):
        gpar = ci % 2
        bpar = e % 2
        if ci + 1 < len(chunks):
            emit_gathers(ci + 1)
        if ci == 0:
            emit_w1(0)
            emit_w2(0)
        for st in range(NSTC):
            for c in range(8):
                par = (st * 8 + c) % 2
                p.op("tensor",
                     lambda E, st=st, c=c, par=par, gpar=gpar: E.transpose(
                         ps_fi[par][:],
                         x1g[gpar][:, st, c * 128:(c + 1) * 128],
                         ident_bf[:]),
                     reads=[("x1g", gpar, st), ("ident",)],
                     writes=[("ps_fi", par)])
                p.op("vector",
                     lambda E, st=st, c=c, par=par: E.tensor_copy(
                         x1gT[:, c, st * 128:(st + 1) * 128],
                         ps_fi[par][:]),
                     reads=[("ps_fi", par)], writes=[("x1gT", st)])
        for f in range(16):
            for n0 in range(0, CH, 512):
                w = min(512, CH - n0)
                par = phh % 2
                phh += 1

                def mmhh(E, f=f, n0=n0, w=w, par=par):
                    for c in range(8):
                        inst = E.matmul(
                            ps_hh[par][:, 0:w],
                            w1_sb[:, c, f * 128:(f + 1) * 128],
                            x1gT[:, c, n0:n0 + w],
                            start=(c == 0), stop=(c == 7))
                    return inst
                p.op("tensor", mmhh,
                     reads=[("w1", f // 8)] +
                           [("x1gT", st) for st in range(n0 // 128,
                                                         (n0 + w) // 128)],
                     writes=[("ps_hh", par)])
                p.op("scalar",
                     lambda E, f=f, n0=n0, w=w, par=par, bpar=bpar:
                     E.activation(hh_sb[:, f, n0:n0 + w],
                                  ps_hh[par][:, 0:w], AF.Gelu,
                                  bias=b1_sb[bpar][:, f:f + 1]),
                     reads=[("ps_hh", par), ("b1", bpar)],
                     writes=[("hh", f)])
        if ch == NCH - 1 and e + 1 < NE:
            emit_w1(e + 1)
        for st in range(NSTC):
            icol = e * (CAP // 128) + ch * NSTC + st
            ypar = pyy % 2
            pyy += 1

            def mmy(E, st=st, ypar=ypar):
                for f in range(16):
                    for n in range(2):
                        inst = E.matmul(
                            ps_y[ypar][:, n * 512:(n + 1) * 512],
                            hh_sb[:, f, st * 128:(st + 1) * 128],
                            w2_sb[:, f, n * 512:(n + 1) * 512],
                            start=(f == 0), stop=(f == 15))
                return inst
            p.op("tensor", mmy,
                 reads=[("hh", f) for f in range(16)] +
                       [("w2", 0), ("w2", 1)],
                 writes=[("ps_out", ypar)])
            p.op("vector",
                 lambda E, icol=icol, ypar=ypar: E.tensor_scalar(
                     yw[ypar][:], ps_y[ypar][:],
                     tw_sb[:, icol, 1:2], None, ALU.mult),
                 reads=[("ps_out", ypar), ("tw",)],
                 writes=[("yw", ypar)])
            p.op("gpsimd",
                 lambda E, icol=icol, ypar=ypar:
                 E.indirect_dma_start(
                     d_accd, IndirectOffsetOnAxis(
                         idx2_i32[:, icol:icol + 1], 0),
                     yw[ypar][:], None),
                 reads=[("yw", ypar), ("idx2",)],
                 writes=[("accw", ci, st)], dma_sem=s_acc[0])
        if ch == NCH - 1 and e + 1 < NE:
            emit_w2(e + 1)

    acc_done = [("accw", ci, st) for ci in range(len(chunks))
                for st in range(NSTC)]
    for tt in range(NTT):
        par = tt % 2
        dma("sync", accl[par][:], d_accB[tt * 128:(tt + 1) * 128, :],
            s_xres[par],
            reads=[("accd_init", tt)], writes=[("accl", par)])
        dma("sync", y1l[par][:], d_accd[tt * 128:(tt + 1) * 128, :],
            s_y1l[par], reads=acc_done, writes=[("y1l", par)])
        dma("sync", y2l[par][:],
            d_accd[(T + 128) + tt * 128:(T + 128) + (tt + 1) * 128, :],
            s_y2l[par], reads=acc_done, writes=[("y2l", par)])
        dma("sync", x1l[par][:], d_x1d[tt * 128:(tt + 1) * 128, :],
            s_x1l[par], reads=[("x1d", tt)], writes=[("x1l", par)])
        p.op("vector",
             lambda E, par=par: E.tensor_tensor(
                 t2_f32[par][:], accl[par][:], y1l[par][:], ALU.add),
             reads=[("accl", par), ("y1l", par)], writes=[("t2", par)])
        p.op("vector",
             lambda E, par=par: E.tensor_tensor(
                 t2_f32[par][:], t2_f32[par][:], y2l[par][:], ALU.add),
             reads=[("t2", par), ("y2l", par)], writes=[("t2", par)])
        p.op("vector",
             lambda E, par=par: E.tensor_tensor(
                 t2_f32[par][:], t2_f32[par][:], x1l[par][:], ALU.add),
             reads=[("t2", par), ("x1l", par)], writes=[("t2", par)])
        p.op("vector",
             lambda E, par=par: E.tensor_scalar(
                 t2_f32[par][:], t2_f32[par][:], 100.0, -100.0,
                 ALU.min, ALU.max),
             reads=[("t2", par)], writes=[("t2", par)])
        p.op("vector",
             lambda E, par=par: E.scalar_tensor_tensor(
                 t2_f32[par][:], t2_f32[par][:], 1.0, x1l[par][:],
                 ALU.mult, ALU.add),
             reads=[("t2", par), ("x1l", par)], writes=[("t2", par)])
        opar = par
        ln_stats_and_apply(ostg[opar][:], ("ostg", opar), par, t2_f32[par],
                           ("t2", par))
        dma("sync", d_out[tt * 128:(tt + 1) * 128, :], ostg[opar][:],
            s_out[opar], reads=[("ostg", opar)])

    p._wait("sync", s_out[0], p.cnt[s_out[0]])
    p._wait("sync", s_out[1], p.cnt[s_out[1]])

    with nc.Block() as block:
        @block.tensor
        def _(E):
            for fn in p.ops["tensor"]:
                fn(E)

        @block.scalar
        def _(E):
            for fn in p.ops["scalar"]:
                fn(E)

        @block.vector
        def _(E):
            for fn in p.ops["vector"]:
                fn(E)

        @block.gpsimd
        def _(E):
            for fn in p.ops["gpsimd"]:
                fn(E)

        @block.sync
        def _(E):
            for fn in p.ops["sync"]:
                fn(E)

    es_prog.close()
    return nc


def make_inputs(T, xb, Wq, bq, Wk, bk, Wv, bv, Wo, bo, gw, w1, b1, w2, b2):
    f = np.float32
    return {
        "x": np.ascontiguousarray(xb, dtype=f),
        "wqT": np.ascontiguousarray(Wq.T, dtype=f),
        "wkT": np.ascontiguousarray(Wk.T, dtype=f),
        "wvT": np.ascontiguousarray(Wv.T, dtype=f),
        "woT": np.ascontiguousarray(Wo.T, dtype=f),
        "bqT": np.ascontiguousarray(np.asarray(bq, f).reshape(8, 128).T),
        "bkT": np.ascontiguousarray(np.asarray(bk, f).reshape(8, 128).T),
        "bv": np.ascontiguousarray(np.asarray(bv, f).reshape(1, D)),
        "bo": np.ascontiguousarray(np.asarray(bo, f).reshape(1, D)),
        "gwT": np.ascontiguousarray(np.asarray(gw, f).T),
        "w1": np.ascontiguousarray(w1, dtype=f),
        "b1T": np.ascontiguousarray(
            np.asarray(b1, f).reshape(NE, 16, 128).transpose(0, 2, 1)),
        "w2": np.ascontiguousarray(w2, dtype=f),
        "b2": np.ascontiguousarray(b2, dtype=f),
        "ident": np.eye(128, dtype=f),
        "tri": np.triu(np.ones((128, 128), f)),
        "onesrow": np.ones((1, 128), f),
        "triu1": np.triu(np.ones((128, 128), f), 1),
        "ebase": (np.arange(NE, dtype=f) * (T // 2)).reshape(1, NE),
        "boundv": np.tile((np.arange(NE, dtype=f) * (T // 2) +
                           (T // 2) - 1).reshape(1, NE), (128, 1)),
        "iotok": np.arange(128, dtype=f)[:, None] +
                 (np.arange(T // 128, dtype=f) * 128)[None, :],
        "tbli": np.tile(np.array([[float(T), 0.0, float(T), 0.0]], f),
                        (NE * (T // 2) + 128, 1)),
    }


def kernel(L0_attn_x, L0_attn_Wq, L0_attn_bq, L0_attn_Wk, L0_attn_bk,
           L0_attn_Wv, L0_attn_bv, L0_attn_Wo, L0_attn_bo,
           L0_ffn_gate_w, L0_ffn_w1, L0_ffn_b1, L0_ffn_w2, L0_ffn_b2):
    T = 4096
    x = np.asarray(L0_attn_x)
    nc = build_nc(T)
    common = None
    in_maps = []
    for b in range(8):
        m = make_inputs(T, x[b], L0_attn_Wq, L0_attn_bq, L0_attn_Wk,
                        L0_attn_bk, L0_attn_Wv, L0_attn_bv, L0_attn_Wo,
                        L0_attn_bo, L0_ffn_gate_w, L0_ffn_w1, L0_ffn_b1,
                        L0_ffn_w2, L0_ffn_b2)
        if common is None:
            common = m
        else:
            for k in m:
                if k != "x":
                    m[k] = common[k]
        in_maps.append(m)
    res = run_bass_kernel_spmd(nc, in_maps, core_ids=list(range(8)))
    return np.stack([res.results[b]["out"] for b in range(8)], axis=0)


# version 18
# speedup vs baseline: 1.0039x; 1.0039x over previous
import contextlib
import numpy as np

import concourse.bass as bass
import concourse.mybir as mybir
from concourse.bass_utils import run_bass_kernel_spmd
from concourse.bass_core_rust import IndirectOffsetOnAxis

F32 = mybir.dt.float32
BF16 = mybir.dt.bfloat16
AF = mybir.ActivationFunctionType
ALU = mybir.AluOpType
AX = mybir.AxisListType
I32 = mybir.dt.int32

D = 1024
NH = 16
HD = 64
NP = 8
NE = 14
FF = 2048
SCALE = 1.0 / 32.0
ESHIFT = -20.0
EPS = 1e-5

ENGINES = ("tensor", "scalar", "vector", "gpsimd", "sync")


class Plan:

    def __init__(self, nc):
        self.nc = nc
        self.ops = {e: [] for e in ENGINES}
        self.cnt = {}
        self.esem = {}
        for e in ENGINES:
            s = nc.alloc_semaphore(f"sem_{e}")
            self.esem[e] = s
            self.cnt[s] = 0
        self.last_w = {}
        self.readers = {}
        self.floors = {e: {} for e in ENGINES}
        self.dma_sems = set()

    def sem(self, name):
        s = self.nc.alloc_semaphore(name)
        self.cnt[s] = 0
        self.dma_sems.add(s)
        return s

    def _wait(self, eng, sem, val):
        if sem in self.dma_sems:
            val = self.cnt[sem]
        if val <= 0:
            return
        fl = self.floors[eng]
        if fl.get(sem, 0) >= val:
            return
        fl[sem] = val
        self.ops[eng].append(lambda E, s=sem, v=val: E.wait_ge(s, v))

    def barrier(self, eng, keys):
        for k in keys:
            w = self.last_w.get(k)
            if w:
                self._wait(eng, *w)
            for s, c in self.readers.get(k, {}).items():
                self._wait(eng, s, c)

    def op(self, eng, build, reads=(), writes=(), dma_sem=None):
        for k in reads:
            w = self.last_w.get(k)
            if w:
                self._wait(eng, *w)
        for k in writes:
            w = self.last_w.get(k)
            if w:
                self._wait(eng, *w)
            for s, c in self.readers.get(k, {}).items():
                self._wait(eng, s, c)
        if dma_sem is not None:
            sem, n = dma_sem, 16
        else:
            sem, n = self.esem[eng], 1
        tgt = self.cnt[sem] + n
        self.cnt[sem] = tgt

        def fn(E, b=build, s=sem, v=n):
            b(E).then_inc(s, v)

        self.ops[eng].append(fn)
        for k in writes:
            self.last_w[k] = (sem, tgt)
            self.readers[k] = {}
        for k in reads:
            self.readers.setdefault(k, {})[sem] = tgt
        return (sem, tgt)


DBG = False


def build_nc(T):
    NTT = T // 128
    NQC = T // 256
    NKT = T // 128
    NTTH = NTT // 2
    NTC = T // 512

    nc = bass.Bass("TRN2", target_bir_lowering=False)

    d_x = nc.dram_tensor("x", [T, D], F32, kind="ExternalInput").ap()
    d_wqT = nc.dram_tensor("wqT", [D, D], F32, kind="ExternalInput").ap()
    d_wkT = nc.dram_tensor("wkT", [D, D], F32, kind="ExternalInput").ap()
    d_wvT = nc.dram_tensor("wvT", [D, D], F32, kind="ExternalInput").ap()
    d_woT = nc.dram_tensor("woT", [D, D], F32, kind="ExternalInput").ap()
    d_bqT = nc.dram_tensor("bqT", [128, 8], F32, kind="ExternalInput").ap()
    d_bkT = nc.dram_tensor("bkT", [128, 8], F32, kind="ExternalInput").ap()
    d_bv = nc.dram_tensor("bv", [1, D], F32, kind="ExternalInput").ap()
    d_bo = nc.dram_tensor("bo", [1, D], F32, kind="ExternalInput").ap()
    d_gwT = nc.dram_tensor("gwT", [D, NE], F32, kind="ExternalInput").ap()
    d_w1 = nc.dram_tensor("w1", [NE, D, FF], F32, kind="ExternalInput").ap()
    d_b1T = nc.dram_tensor("b1T", [NE, 128, 16], F32, kind="ExternalInput").ap()
    d_w2 = nc.dram_tensor("w2", [NE, FF, D], F32, kind="ExternalInput").ap()
    d_b2 = nc.dram_tensor("b2", [NE, D], F32, kind="ExternalInput").ap()
    d_ident = nc.dram_tensor("ident", [128, 128], F32, kind="ExternalInput").ap()
    d_triu1 = nc.dram_tensor("triu1", [128, 128], F32, kind="ExternalInput").ap()
    d_ebase = nc.dram_tensor("ebase", [1, NE], F32, kind="ExternalInput").ap()
    d_bound = nc.dram_tensor("boundv", [128, NE], F32,
                             kind="ExternalInput").ap()
    d_iotok = nc.dram_tensor("iotok", [128, T // 128], F32,
                             kind="ExternalInput").ap()
    d_tbli = nc.dram_tensor("tbli", [NE * (T // 2) + 128, 4], F32,
                            kind="ExternalInput").ap()
    d_tri = nc.dram_tensor("tri", [128, 128], F32, kind="ExternalInput").ap()
    d_ones = nc.dram_tensor("onesrow", [1, 128], F32, kind="ExternalInput").ap()
    d_out = nc.dram_tensor("out", [T, D], F32, kind="ExternalOutput").ap()
    CAP = T // 2
    NIC = NE * CAP // 128
    d_x1d = nc.dram_tensor("x1d", [T + 128, D], BF16, kind="Internal").ap()
    d_accB = nc.dram_tensor("accB", [T, D], F32, kind="Internal").ap()
    d_accd = nc.dram_tensor("accd", [2 * (T + 128), D], BF16,
                            kind="Internal").ap()
    d_tbl = nc.dram_tensor("tbl", [NE * CAP + 128, 4], F32,
                           kind="Internal").ap()

    p = Plan(nc)
    s_init = p.sem("s_init")
    s_init2 = p.sem("s_init2")
    s_xbf = [p.sem("s_xbf0"), p.sem("s_xbf1")]
    s_wsl = [p.sem("s_wq"), p.sem("s_wk"), p.sem("s_wv")]
    s_wo = p.sem("s_wo")
    s_xres = [p.sem("s_xres0"), p.sem("s_xres1")]
    s_w1 = p.sem("s_w1")
    s_w2 = p.sem("s_w2")
    s_bb = [p.sem("s_bb0"), p.sem("s_bb1")]
    s_out = [p.sem("s_out0"), p.sem("s_out1")]
    s_binit = p.sem("s_binit")
    s_tblw = p.sem("s_tblw")
    s_gx = [p.sem("s_gx0"), p.sem("s_gx1")]
    s_sc = p.sem("s_sc")
    s_acc = [p.sem("s_acc0"), p.sem("s_acc1")]
    s_x1l = [p.sem("s_x1l0"), p.sem("s_x1l1")]
    s_y1l = [p.sem("s_y1l0"), p.sem("s_y1l1")]
    s_y2l = [p.sem("s_y2l0"), p.sem("s_y2l1")]

    es_prog = contextlib.ExitStack()
    alloc = lambda es, *a, **kw: es.enter_context(nc.sbuf_tensor(*a, **kw))

    ident_bf = alloc(es_prog, "ident_bf", [128, 128], BF16)
    tri_bf = alloc(es_prog, "tri_bf", [128, 128], BF16)
    ones_bf = alloc(es_prog, "ones_bf", [1, 128], BF16)
    bqT_sb = alloc(es_prog, "bqT_sb", [128, 8], F32)
    bkT_sb = alloc(es_prog, "bkT_sb", [128, 8], F32)
    maskw = alloc(es_prog, "maskw", [128, NTT, NE], F32)


    onorm = [alloc(es_prog, f"onorm{i}", [128, 64], BF16) for i in range(8)]
    rd8 = alloc(es_prog, "rd8", [128, 8], F32)
    sc1p = [{k: alloc(es_prog, f"sc_{k}_{i}", [128, 1], F32)
             for k in ("m1", "m2", "w0", "w1c", "ee")} for i in range(2)]
    scL = [{k: alloc(es_prog, f"scL_{k}_{i}", [128, 1], F32)
            for k in ("sums", "sumsq", "mean", "em2", "var", "rstd", "nmr")}
           for i in range(2)]
    sq_f32 = alloc(es_prog, "sq_f32", [128, D], F32)
    scEp = [{k: alloc(es_prog, f"scE_{k}_{i}", [128, NE], F32)
             for k in ("lg", "eq1", "eq2", "tmp")} for i in range(2)]
    cst = alloc(es_prog, "cst", [128, 4], F32)
    c_scale = cst[:, 0:1]
    c_eshift = cst[:, 1:2]
    c_mhalf = cst[:, 2:3]
    c_eps = cst[:, 3:4]

    es_act = contextlib.ExitStack()
    actT = alloc(es_act, "actT", [128, 8, T], BF16)

    es_ps_att = contextlib.ExitStack()
    pal = lambda es, *a, **kw: es.enter_context(nc.psum_tensor(*a, **kw))
    ps_st0 = pal(es_ps_att, "ps_st0", [128, 8, 128], F32)
    ps_st1 = pal(es_ps_att, "ps_st1", [128, 8, 128], F32)
    ps_o = [pal(es_ps_att, f"ps_o{i}", [128, 65], F32) for i in range(2)]
    ps_tr = [pal(es_ps_att, f"ps_tr{i}", [128, 128], BF16) for i in range(2)]
    _st_flat = [ps_st0[:].rearrange("p a b -> p (a b)"),
                ps_st1[:].rearrange("p a b -> p (a b)")]
    ps_proj = [_st_flat[0][:, 0:512], _st_flat[0][:, 512:1024],
               _st_flat[1][:, 0:512], _st_flat[1][:, 512:1024]]
    ps_proj_key = [("pstb", 0), ("pstb", 1), ("pstb", 2), ("pstb", 3)]

    es_b1 = contextlib.ExitStack()
    concatT = alloc(es_b1, "concatT", [128, 8, T], BF16)

    es_b2 = contextlib.ExitStack()
    QTp = alloc(es_b2, "QTp", [128, T], BF16)
    KTp = alloc(es_b2, "KTp", [128, T], BF16)
    Vp = alloc(es_b2, "Vp", [128, NTT, 2, 66], BF16)
    wq_sl = alloc(es_b2, "wq_sl", [128, 8, 128], BF16)
    wk_sl = alloc(es_b2, "wk_sl", [128, 8, 128], BF16)
    wv_sl = alloc(es_b2, "wv_sl", [128, 8, 128], BF16)
    bv_row = alloc(es_b2, "bv_row", [1, D], BF16)
    es_b2x = contextlib.ExitStack()
    xbf = [alloc(es_b2x, f"xbf{i}", [128, D], BF16) for i in range(2)]
    es_b2x.close()
    A0 = [alloc(es_b2, f"A0_{i}", [128, NKT, 128], BF16) for i in range(2)]
    A1 = [alloc(es_b2, f"A1_{i}", [128, NKT, 128], BF16) for i in range(2)]

    def dma(eng, out_ap, in_ap, sem, reads=(), writes=()):
        p.op(eng, lambda E, o=out_ap, i=in_ap: E.dma_start(o, i),
             reads=reads, writes=writes, dma_sem=sem)

    dma("gpsimd", ident_bf[:], d_ident, s_init, writes=[("ident",)])
    dma("gpsimd", tri_bf[:], d_tri, s_init, writes=[("tri",)])
    dma("gpsimd", ones_bf[:], d_ones, s_init, writes=[("ones",)])
    dma("sync", bqT_sb[:], d_bqT, s_init2, writes=[("bqT",)])
    dma("sync", bkT_sb[:], d_bkT, s_init2, writes=[("bkT",)])
    dma("gpsimd", bv_row[:], d_bv, s_init, writes=[("bv",)])
    for w_sl, d_w, si, nm in ((wq_sl, d_wqT, 0, "wq"), (wk_sl, d_wkT, 1, "wk"),
                              (wv_sl, d_wvT, 2, "wv")):
        dma("gpsimd", w_sl[:],
            d_w[:, 0:128].rearrange("(c q) m -> q c m", q=128),
            s_wsl[si], writes=[(nm,)])
    p.op("vector", lambda E: E.memset(Vp[:, :, :, 64:65], 1.0),
         writes=[("Vpones",)])
    dma("gpsimd", d_tbl, d_tbli, s_init, writes=[("tbl",)])
    for i, v in enumerate((SCALE, ESHIFT, -0.5, EPS)):
        p.op("vector", lambda E, i=i, v=v: E.memset(cst[:, i:i + 1], v),
             writes=[("cst",)])

    for tt in range(NTT):
        par = tt % 2
        dma("gpsimd", xbf[par][:], d_x[tt * 128:(tt + 1) * 128, :],
            s_xbf[par], writes=[("xbf", par)])
        for c in range(8):
            p.op("tensor",
                 lambda E, par=par, c=c: E.transpose(
                     ps_tr[c % 2][:], xbf[par][:, c * 128:(c + 1) * 128],
                     ident_bf[:]),
                 reads=[("xbf", par), ("ident",)], writes=[("ps_tr", c % 2)])
            p.op("vector",
                 lambda E, par=par, c=c, tt=tt: E.tensor_copy(
                     actT[:, c, tt * 128:(tt + 1) * 128], ps_tr[c % 2][:]),
                 reads=[("ps_tr", c % 2)], writes=[("actT", c, tt)])

    pp = 0
    po = [0]
    pt = [0]
    for pair in range(NP):
        for (w_sl, b_sb, dst, wkey, bkey, dkey) in (
                (wq_sl, bqT_sb, QTp, "wq", "bqT", "QTp"),
                (wk_sl, bkT_sb, KTp, "wk", "bkT", "KTp")):
            for tc in range(NTC):
                ps = ps_proj[pp % 4]
                pskey = ps_proj_key[pp % 4]
                pp += 1

                def mmg(E, w_sl=w_sl, tc=tc, ps=ps):
                    for c in range(8):
                        inst = E.matmul(ps, w_sl[:, c, :],
                                        actT[:, c, tc * 512:(tc + 1) * 512],
                                        start=(c == 0), stop=(c == 7))
                    return inst
                p.op("tensor", mmg,
                     reads=[(wkey,)] + [("actT", c, 4 * tc + j)
                                        for c in range(8) for j in range(4)],
                     writes=[pskey])
                p.op("vector",
                     lambda E, dst=dst, tc=tc, ps=ps, b_sb=b_sb, pair=pair:
                     E.tensor_scalar(dst[:, tc * 512:(tc + 1) * 512], ps,
                                     b_sb[:, pair:pair + 1], None, ALU.add),
                     reads=[pskey, (bkey,)], writes=[(dkey, tc)])
        for tt in range(NTT):
            ps = ps_proj[pp % 4]
            pskey = ps_proj_key[pp % 4]
            pp += 1

            def mmv(E, tt=tt, ps=ps, pair=pair):
                E.matmul(ps[:, 0:128], ones_bf[:],
                         bv_row[0:1, pair * 128:(pair + 1) * 128],
                         start=True, stop=False)
                for c in range(8):
                    inst = E.matmul(ps[:, 0:128],
                                    actT[:, c, tt * 128:(tt + 1) * 128],
                                    wv_sl[:, c, :],
                                    start=False, stop=(c == 7))
                return inst
            p.op("tensor", mmv,
                 reads=[("wv",), ("ones",), ("bv",)] +
                       [("actT", c, tt) for c in range(8)],
                 writes=[pskey])
            p.op("vector",
                 lambda E, tt=tt, ps=ps: E.tensor_copy(
                     Vp[:, tt, 0:2, 0:64],
                     ps[:, 0:128].rearrange("q (h f) -> q h f", h=2)),
                 reads=[pskey], writes=[("Vp", tt)])

        if pair + 1 < NP:
            for w_sl, d_w, si, nm in ((wq_sl, d_wqT, 0, "wq"),
                                      (wk_sl, d_wkT, 1, "wk"),
                                      (wv_sl, d_wvT, 2, "wv")):
                dma("gpsimd", w_sl[:],
                    d_w[:, (pair + 1) * 128:(pair + 2) * 128]
                    .rearrange("(c q) m -> q c m", q=128),
                    s_wsl[si], writes=[(nm,)])


        def st_span_ops(qt):
            apar = qt % 2
            spans = [(8 * s, min(8, qt + 1 - 8 * s))
                     for s in range((qt + 1 + 7) // 8)]

            def one(h, k0, nk, qt=qt, apar=apar):
                A_h, ps_h, pk0, p0, tp = (
                    (A0, ps_st0, 0, 0, (0, 0)),
                    (A1, ps_st1, 2, 64, (64, 0)))[h]

                def mmst(E, ps_h=ps_h, p0=p0, tp=tp, qt=qt, k0=k0, nk=nk):
                    for j in range(nk):
                        kt = k0 + j
                        inst = E.matmul(
                            ps_h[:, j, :],
                            KTp[p0:p0 + 64, kt * 128:(kt + 1) * 128],
                            QTp[p0:p0 + 64, qt * 128:(qt + 1) * 128],
                            start=True, stop=True, tile_position=tp)
                    return inst
                p.op("tensor", mmst,
                     reads=[("QTp", qt // 4), ("KTp", k0 // 4),
                            ("KTp", (k0 + nk - 1) // 4)],
                     writes=[("pstb", pk0), ("pstb", pk0 + 1)])
                p.op("scalar",
                     lambda E, A_h=A_h, ps_h=ps_h, k0=k0, nk=nk, apar=apar:
                     E.activation(A_h[apar][:, k0:k0 + nk, :],
                                  ps_h[:, 0:nk, :], AF.Exp,
                                  bias=c_eshift, scale=c_scale),
                     reads=[("pstb", pk0), ("pstb", pk0 + 1), ("cst",)],
                     writes=[("A", h, apar, k0 + j) for j in range(nk)])

            ops = []
            for i in range(len(spans) + 1):
                def emit(i=i, spans=spans):
                    if i > 0:
                        one(1, *spans[i - 1])
                    if i < len(spans):
                        one(0, *spans[i])
                ops.append(emit)

            def emit_mask(qt=qt, apar=apar):
                for h, A_h in enumerate((A0, A1)):
                    p.op("vector",
                         lambda E, A_h=A_h, apar=apar, qt=qt: E.tensor_tensor(
                             A_h[apar][:, qt, :], A_h[apar][:, qt, :],
                             tri_bf[:], ALU.mult),
                         reads=[("A", h, apar, qt), ("tri",)],
                         writes=[("A", h, apar, qt)])
            ops.append(emit_mask)
            return ops

        def av_ops(qt):
            apar = qt % 2
            ops = []
            fini = []
            for h, A_h in enumerate((A0, A1)):
                opar = po[0] % 2
                po[0] += 1
                slot = (2 * qt + h) % 8
                nmm = qt + 1
                nch = max(1, (nmm + 7) // 8)
                bounds = [(i * nmm // nch, (i + 1) * nmm // nch)
                          for i in range(nch)]
                for ci, (a, bnd) in enumerate(bounds):
                    def emit(A_h=A_h, h=h, qt=qt, apar=apar, opar=opar,
                             a=a, bnd=bnd):
                        def mmav(E):
                            for kt in range(a, bnd):
                                inst = E.matmul(ps_o[opar][:],
                                                A_h[apar][:, kt, :],
                                                Vp[:, kt, h, 0:65],
                                                start=(kt == 0),
                                                stop=(kt == qt))
                            return inst
                        p.op("tensor", mmav,
                             reads=[("A", h, apar, kt)
                                    for kt in range(a, bnd)] +
                                   [("Vp", kt) for kt in range(a, bnd)] +
                                   [("Vpones",)],
                             writes=[("ps_o", opar)])
                    ops.append(emit)

                def emit_norm(h=h, qt=qt, opar=opar, slot=slot):
                    p.op("vector",
                         lambda E, opar=opar, slot=slot: E.reciprocal(
                             rd8[:, slot:slot + 1], ps_o[opar][:, 64:65]),
                         reads=[("ps_o", opar)], writes=[("rd", slot)])
                    p.op("vector",
                         lambda E, opar=opar, slot=slot: E.tensor_scalar(
                             onorm[slot][:], ps_o[opar][:, 0:64],
                             rd8[:, slot:slot + 1], None, ALU.mult),
                         reads=[("ps_o", opar), ("rd", slot)],
                         writes=[("onorm", slot)])
                ops.append(emit_norm)

                def emit_tr(h=h, qt=qt, slot=slot):
                    tpar = pt[0] % 2
                    pt[0] += 1
                    p.op("tensor",
                         lambda E, slot=slot, tpar=tpar: E.transpose(
                             ps_tr[tpar][0:64, :], onorm[slot][:],
                             ident_bf[:]),
                         reads=[("onorm", slot), ("ident",)],
                         writes=[("ps_tr", tpar)])
                    p.op("vector",
                         lambda E, h=h, qt=qt, tpar=tpar, pair=pair:
                         E.tensor_copy(concatT[h * 64:(h + 1) * 64, pair,
                                               qt * 128:(qt + 1) * 128],
                                       ps_tr[tpar][0:64, :]),
                         reads=[("ps_tr", tpar)],
                         writes=[("cat", pair, qt)])
                fini.append(emit_tr)
            return ops, fini

        prev_av = []
        pending_tr = []
        for qt in range(NKT):
            st = st_span_ops(qt)
            n = max(len(st), len(prev_av))
            for i in range(n):
                if i < len(st):
                    st[i]()
                if i < len(prev_av):
                    prev_av[i]()
            while len(pending_tr) > 6:
                pending_tr.pop(0)()
            prev_av, fin = av_ops(qt)
            pending_tr.extend(fin)
        for e in prev_av:
            e()
        for f in pending_tr:
            f()

    attdead = ([("QTp", i) for i in range(NTC)] +
               [("KTp", i) for i in range(NTC)] +
               [("Vp", tt) for tt in range(NTT)] + [("Vpones",)] +
               [("A", h, kt) for h in range(2) for kt in range(NKT)] +
               [("wq",), ("wk",), ("wv",), ("bv",),
                ("xbf", 0), ("xbf", 1)])
    es_b2.close()
    es_c = contextlib.ExitStack()
    woT_sb = alloc(es_c, "woT_sb", [128, 8, D], BF16)
    bo_row = alloc(es_c, "bo_row", [1, D], BF16)
    xres = [alloc(es_c, f"xres{i}", [128, D], F32) for i in range(2)]
    t_f32 = [alloc(es_c, f"t_f32{i}", [128, D], F32) for i in range(2)]
    x1n = [alloc(es_c, f"x1n{i}", [128, D], BF16) for i in range(2)]
    gwT_sb = alloc(es_c, "gwT_sb", [128, 8, NE], BF16)
    b2_all = alloc(es_c, "b2_all", [14, D], BF16)
    mwT = alloc(es_c, "mwT", [14, NTT, 128], BF16)
    mwb16p = [alloc(es_c, f"mwb16_{i}", [128, NE], BF16) for i in range(2)]
    iotok_sb = alloc(es_c, "iotok_sb", [128, T // 128], F32)
    triu1_bf = alloc(es_c, "triu1_bf", [128, 128], BF16)
    onescol = alloc(es_c, "onescol", [128, 1], BF16)
    ones_f32 = alloc(es_c, "ones_f32", [1, 128], F32)
    base_row = alloc(es_c, "base_row", [1, NE], F32)
    m_bfp = [alloc(es_c, f"m_bf{i}", [128, NE], BF16) for i in range(2)]
    bsbp = [alloc(es_c, f"bsb{i}", [128, NE], F32) for i in range(2)]
    sf3p = [alloc(es_c, f"sf3{i}", [128, NE], F32) for i in range(2)]
    boundc = alloc(es_c, "boundc", [128, NE], F32)
    prp = [[alloc(es_c, f"pr{i}_{j}", [128, 4], F32) for j in range(2)]
           for i in range(2)]
    fsip = [[alloc(es_c, f"fsi{i}_{j}", [128, 1], I32) for j in range(2)]
            for i in range(2)]

    p.barrier("gpsimd", attdead)
    dma("gpsimd", woT_sb[:], d_woT.rearrange("(c q) m -> q c m", q=128),
        s_wo, writes=[("woT",)])
    dma("gpsimd", bo_row[:], d_bo, s_wo, writes=[("bo",)])
    dma("gpsimd", gwT_sb[:], d_gwT.rearrange("(c q) m -> q c m", q=128),
        s_wo, writes=[("gwT",)])
    dma("gpsimd", b2_all[:], d_b2, s_wo, writes=[("b2a",)])
    dma("gpsimd", triu1_bf[:], d_triu1, s_wo, writes=[("triu1",)])
    p.barrier("sync", attdead)
    dma("sync", base_row[:], d_ebase, s_init2, writes=[("base",)])
    dma("sync", boundc[:], d_bound, s_init2, writes=[("boundc",)])
    dma("sync", iotok_sb[:], d_iotok, s_init2, writes=[("iotok",)])
    p.op("vector", lambda E: E.memset(onescol[:], 1.0), writes=[("onescol",)])
    p.op("vector", lambda E: E.memset(ones_f32[:], 1.0), writes=[("onesf",)])
    for i in range(2):
        for j in range(2):
            p.op("vector", lambda E, i=i, j=j: E.memset(prp[i][j][:, 3:4],
                                                        0.0),
                 writes=[("prpad", i, j)])

    attps = ([("pstb", i) for i in range(4)] +
             [("ps_o", i) for i in range(2)] + [("ps_tr", i) for i in range(2)])
    es_ps_att.close()
    ps_out = [pal(es_prog, f"ps_out{i}", [128, D], F32) for i in range(2)]
    es_ps_3 = contextlib.ExitStack()
    ps_g = [pal(es_ps_3, f"ps_g{i}", [128, 128], F32) for i in range(2)]
    ps_t3 = [pal(es_ps_3, f"ps_t3{i}", [128, 128], BF16) for i in range(2)]
    p.barrier("tensor", attps)

    def ln_stats_and_apply(out_ap, out_key, par, tbuf, tkey):
        SC = scL[par]

        def k(name):
            return (name, par)
        p.op("vector",
             lambda E, tbuf=tbuf: E.reduce_sum(SC["sums"][:], tbuf[:],
                                               axis=AX.X),
             reads=[tkey], writes=[k("sums")])
        p.op("scalar",
             lambda E, tbuf=tbuf: E.activation(sq_f32[:], tbuf[:],
                                               AF.Square),
             reads=[tkey], writes=[("sq",)])
        p.op("vector",
             lambda E: E.reduce_sum(SC["sumsq"][:], sq_f32[:], axis=AX.X),
             reads=[("sq",)], writes=[k("sumsq")])
        p.op("vector",
             lambda E: E.tensor_scalar(SC["mean"][:], SC["sums"][:],
                                       1.0 / D, None, ALU.mult),
             reads=[k("sums")], writes=[k("mean")])
        p.op("vector",
             lambda E: E.tensor_scalar(SC["em2"][:], SC["sumsq"][:],
                                       1.0 / D, None, ALU.mult),
             reads=[k("sumsq")], writes=[k("em2")])
        p.op("vector",
             lambda E: E.tensor_tensor(SC["var"][:], SC["mean"][:],
                                       SC["mean"][:], ALU.mult),
             reads=[k("mean")], writes=[k("var")])
        p.op("vector",
             lambda E: E.tensor_sub(SC["var"][:], SC["em2"][:], SC["var"][:]),
             reads=[k("em2"), k("var")], writes=[k("var")])
        p.op("scalar",
             lambda E: E.activation(SC["rstd"][:], SC["var"][:], AF.Ln,
                                    bias=c_eps),
             reads=[k("var"), ("cst",)], writes=[k("rstd")])
        p.op("scalar",
             lambda E: E.activation(SC["rstd"][:], SC["rstd"][:], AF.Exp,
                                    scale=c_mhalf),
             reads=[k("rstd"), ("cst",)], writes=[k("rstd")])
        p.op("vector",
             lambda E: E.scalar_tensor_tensor(SC["nmr"][:], SC["mean"][:],
                                              -1.0, SC["rstd"][:],
                                              ALU.mult, ALU.mult),
             reads=[k("mean"), k("rstd")], writes=[k("nmr")])
        p.op("vector",
             lambda E, out_ap=out_ap, tbuf=tbuf: E.tensor_scalar(
                 out_ap, tbuf[:], SC["rstd"][:], SC["nmr"][:],
                 ALU.mult, ALU.add),
             reads=[tkey, k("rstd"), k("nmr")], writes=[out_key])

    dma("sync", xres[0][:], d_x[0:128, :], s_xres[0], writes=[("xres", 0)])
    for tt in range(NTT):
        par = tt % 2
        if tt + 1 < NTT:
            dma("sync", xres[(tt + 1) % 2][:],
                d_x[(tt + 1) * 128:(tt + 2) * 128, :],
                s_xres[(tt + 1) % 2], writes=[("xres", (tt + 1) % 2)])

        def mmo(E, tt=tt, par=par):
            E.matmul(ps_out[par][:, 0:512], ones_bf[:], bo_row[0:1, 0:512],
                     start=True, stop=False)
            E.matmul(ps_out[par][:, 512:1024], ones_bf[:],
                     bo_row[0:1, 512:1024], start=True, stop=False)
            for c in range(8):
                for n in range(2):
                    inst = E.matmul(ps_out[par][:, n * 512:(n + 1) * 512],
                                    concatT[:, c, tt * 128:(tt + 1) * 128],
                                    woT_sb[:, c, n * 512:(n + 1) * 512],
                                    start=False, stop=(c == 7))
            return inst
        p.op("tensor", mmo,
             reads=[("woT",), ("bo",), ("ones",)] +
                   [("cat", c, tt) for c in range(8)],
             writes=[("ps_out", par)])
        p.op("vector",
             lambda E, par=par: E.tensor_tensor(
                 t_f32[par][:], ps_out[par][:], xres[par][:], ALU.add),
             reads=[("ps_out", par), ("xres", par)], writes=[("t", par)])
        ln_stats_and_apply(x1n[par][:], ("x1n", par), par, t_f32[par],
                           ("t", par))
        dma("sync", d_x1d[tt * 128:(tt + 1) * 128, :], x1n[par][:],
            s_xres[par], reads=[("x1n", par)], writes=[("x1d", tt)])
        for c in range(8):
            p.op("tensor",
                 lambda E, par=par, c=c: E.transpose(
                     ps_t3[c % 2][:], x1n[par][:, c * 128:(c + 1) * 128],
                     ident_bf[:]),
                 reads=[("x1n", par), ("ident",)], writes=[("ps_t3", c % 2)])
            p.op("vector",
                 lambda E, c=c, tt=tt: E.tensor_copy(
                     actT[:, c, tt * 128:(tt + 1) * 128], ps_t3[c % 2][:]),
                 reads=[("ps_t3", c % 2)], writes=[("actT", c, tt)])

        sc1 = sc1p[par]
        scE = scEp[par]
        m_bf = m_bfp[par]
        bsb = bsbp[par]
        sf3 = sf3p[par]
        pr = prp[par]
        fsi = fsip[par]
        mwb16 = mwb16p[par]

        def mmgate(E, tt=tt, par=par):
            for c in range(8):
                inst = E.matmul(ps_g[par][:, 0:NE],
                                actT[:, c, tt * 128:(tt + 1) * 128],
                                gwT_sb[:, c, :],
                                start=(c == 0), stop=(c == 7))
            return inst
        p.op("tensor", mmgate,
             reads=[("gwT",)] + [("actT", c, tt) for c in range(8)],
             writes=[("ps_g", par)])
        p.op("vector",
             lambda E, par=par, sc1=sc1, scE=scE, pr=pr, fsi=fsi, m_bf=m_bf, bsb=bsb, sf3=sf3, mwb16=mwb16: E.tensor_copy(scE["lg"][:], ps_g[par][:, 0:NE]),
             reads=[("ps_g", par)], writes=[("lg", par)])
        p.op("vector",
             lambda E, sc1=sc1, scE=scE, pr=pr, fsi=fsi, m_bf=m_bf, bsb=bsb, sf3=sf3, mwb16=mwb16: E.reduce_max(sc1["m1"][:], scE["lg"][:], axis=AX.X),
             reads=[("lg", par)], writes=[("m1", par)])
        p.op("vector",
             lambda E, sc1=sc1, scE=scE, pr=pr, fsi=fsi, m_bf=m_bf, bsb=bsb, sf3=sf3, mwb16=mwb16: E.tensor_scalar(scE["eq1"][:], scE["lg"][:],
                                       sc1["m1"][:], None, ALU.is_ge),
             reads=[("lg", par), ("m1", par)], writes=[("eq1", par)])
        p.op("vector",
             lambda E, sc1=sc1, scE=scE, pr=pr, fsi=fsi, m_bf=m_bf, bsb=bsb, sf3=sf3, mwb16=mwb16: E.scalar_tensor_tensor(scE["tmp"][:], scE["eq1"][:],
                                              -1e9, scE["lg"][:],
                                              ALU.mult, ALU.add),
             reads=[("eq1", par), ("lg", par)], writes=[("tmpE", par)])
        p.op("vector",
             lambda E, sc1=sc1, scE=scE, pr=pr, fsi=fsi, m_bf=m_bf, bsb=bsb, sf3=sf3, mwb16=mwb16: E.reduce_max(sc1["m2"][:], scE["tmp"][:], axis=AX.X),
             reads=[("tmpE", par)], writes=[("m2", par)])
        p.op("vector",
             lambda E, sc1=sc1, scE=scE, pr=pr, fsi=fsi, m_bf=m_bf, bsb=bsb, sf3=sf3, mwb16=mwb16: E.tensor_scalar(scE["eq2"][:], scE["lg"][:],
                                       sc1["m2"][:], None, ALU.is_ge),
             reads=[("lg", par), ("m2", par)], writes=[("eq2", par)])
        p.op("vector",
             lambda E, sc1=sc1, scE=scE, pr=pr, fsi=fsi, m_bf=m_bf, bsb=bsb, sf3=sf3, mwb16=mwb16: E.tensor_sub(sc1["ee"][:], sc1["m2"][:], sc1["m1"][:]),
             reads=[("m1", par), ("m2", par)], writes=[("ee", par)])
        p.op("scalar",
             lambda E, sc1=sc1, scE=scE, pr=pr, fsi=fsi, m_bf=m_bf, bsb=bsb, sf3=sf3, mwb16=mwb16: E.activation(sc1["ee"][:], sc1["ee"][:], AF.Exp),
             reads=[("ee", par)], writes=[("ee", par)])
        p.op("vector",
             lambda E, sc1=sc1, scE=scE, pr=pr, fsi=fsi, m_bf=m_bf, bsb=bsb, sf3=sf3, mwb16=mwb16: E.tensor_scalar(sc1["ee"][:], sc1["ee"][:], 1.0, None,
                                       ALU.add),
             reads=[("ee", par)], writes=[("ee", par)])
        p.op("vector",
             lambda E, sc1=sc1, scE=scE, pr=pr, fsi=fsi, m_bf=m_bf, bsb=bsb, sf3=sf3, mwb16=mwb16: E.reciprocal(sc1["w0"][:], sc1["ee"][:]),
             reads=[("ee", par)], writes=[("w0", par)])
        p.op("vector",
             lambda E, sc1=sc1, scE=scE, pr=pr, fsi=fsi, m_bf=m_bf, bsb=bsb, sf3=sf3, mwb16=mwb16: E.tensor_scalar(sc1["w1c"][:], sc1["w0"][:], -1.0, 1.0,
                                       ALU.mult, ALU.add),
             reads=[("w0", par)], writes=[("w1c", par)])
        p.op("vector",
             lambda E, sc1=sc1, scE=scE, pr=pr, fsi=fsi, m_bf=m_bf, bsb=bsb, sf3=sf3, mwb16=mwb16: E.tensor_tensor(scE["tmp"][:], scE["eq2"][:],
                                       scE["eq1"][:], ALU.mult),
             reads=[("eq1", par), ("eq2", par)], writes=[("tmpE", par)])
        p.op("vector",
             lambda E, sc1=sc1, scE=scE, pr=pr, fsi=fsi, m_bf=m_bf, bsb=bsb, sf3=sf3, mwb16=mwb16: E.tensor_sub(scE["tmp"][:], scE["eq2"][:],
                                    scE["tmp"][:]),
             reads=[("eq2", par), ("tmpE", par)], writes=[("tmpE", par)])
        p.op("vector",
             lambda E, sc1=sc1, scE=scE, pr=pr, fsi=fsi, m_bf=m_bf, bsb=bsb, sf3=sf3, mwb16=mwb16: E.tensor_scalar(scE["tmp"][:], scE["tmp"][:],
                                       sc1["w1c"][:], None, ALU.mult),
             reads=[("tmpE", par), ("w1c", par)], writes=[("tmpE", par)])
        p.op("vector",
             lambda E, sc1=sc1, scE=scE, pr=pr, fsi=fsi, m_bf=m_bf, bsb=bsb, sf3=sf3, mwb16=mwb16: E.tensor_scalar(scE["eq1"][:], scE["eq1"][:],
                                       sc1["w0"][:], None, ALU.mult),
             reads=[("eq1", par), ("w0", par)], writes=[("eq1", par)])
        p.op("vector",
             lambda E, tt=tt, sc1=sc1, scE=scE, pr=pr, fsi=fsi, m_bf=m_bf, bsb=bsb, sf3=sf3, mwb16=mwb16: E.tensor_tensor(maskw[:, tt, :], scE["eq1"][:],
                                              scE["tmp"][:], ALU.add),
             reads=[("eq1", par), ("tmpE", par)], writes=[("mw", tt)])
        p.op("vector",
             lambda E, tt=tt, sc1=sc1, scE=scE, pr=pr, fsi=fsi, m_bf=m_bf, bsb=bsb, sf3=sf3, mwb16=mwb16: E.tensor_copy(mwb16[:], maskw[:, tt, :]),
             reads=[("mw", tt)], writes=[("mwb", par)])
        p.op("tensor",
             lambda E, par=par, sc1=sc1, scE=scE, pr=pr, fsi=fsi, m_bf=m_bf, bsb=bsb, sf3=sf3, mwb16=mwb16: E.transpose(ps_t3[par][0:14, :], mwb16[:],
                                            ident_bf[:]),
             reads=[("mwb", par), ("ident",)], writes=[("ps_t3", par)])
        p.op("vector",
             lambda E, tt=tt, par=par, sc1=sc1, scE=scE, pr=pr, fsi=fsi, m_bf=m_bf, bsb=bsb, sf3=sf3, mwb16=mwb16: E.tensor_copy(mwT[:, tt, :],
                                                     ps_t3[par][0:14, :]),
             reads=[("ps_t3", par)], writes=[("mwT", tt)])
        def mmb2(E, tt=tt, par=par):
            for n in range(2):
                inst = E.matmul(ps_out[par][:, n * 512:(n + 1) * 512],
                                mwT[:, tt, :],
                                b2_all[:, n * 512:(n + 1) * 512],
                                start=True, stop=True)
            return inst
        p.op("tensor", mmb2, reads=[("mwT", tt), ("b2a",)],
             writes=[("ps_out", par)])
        p.op("vector",
             lambda E, par=par, sc1=sc1, scE=scE, pr=pr, fsi=fsi, m_bf=m_bf, bsb=bsb, sf3=sf3, mwb16=mwb16: E.tensor_copy(t_f32[par][:], ps_out[par][:]),
             reads=[("ps_out", par)], writes=[("t", par)])
        dma("sync", d_accB[tt * 128:(tt + 1) * 128, :], t_f32[par][:],
            s_out[par], reads=[("t", par)], writes=[("accd_init", tt)])
        p.op("vector",
             lambda E, tt=tt, sc1=sc1, scE=scE, pr=pr, fsi=fsi, m_bf=m_bf, bsb=bsb, sf3=sf3, mwb16=mwb16: E.tensor_scalar(m_bf[:], maskw[:, tt, :], 0.0,
                                              None, ALU.is_gt),
             reads=[("mw", tt)], writes=[("memb", par)])
        p.op("tensor",
             lambda E, sc1=sc1, scE=scE, pr=pr, fsi=fsi, m_bf=m_bf, bsb=bsb, sf3=sf3, mwb16=mwb16: E.matmul(ps_g[0][:, 0:NE], triu1_bf[:], m_bf[:],
                                start=True, stop=True),
             reads=[("triu1",), ("memb", par)], writes=[("ps_g", 0)])
        p.op("tensor",
             lambda E, sc1=sc1, scE=scE, pr=pr, fsi=fsi, m_bf=m_bf, bsb=bsb, sf3=sf3, mwb16=mwb16: E.matmul(ps_g[1][:, 0:NE], ones_f32[:], base_row[:],
                                start=True, stop=True),
             reads=[("onesf",), ("base",)], writes=[("ps_g", 1)])
        p.op("vector",
             lambda E, sc1=sc1, scE=scE, pr=pr, fsi=fsi, m_bf=m_bf, bsb=bsb, sf3=sf3, mwb16=mwb16: E.tensor_copy(bsb[:], ps_g[1][:, 0:NE]),
             reads=[("ps_g", 1)], writes=[("bsb", par)])
        p.op("tensor",
             lambda E, sc1=sc1, scE=scE, pr=pr, fsi=fsi, m_bf=m_bf, bsb=bsb, sf3=sf3, mwb16=mwb16: E.matmul(ps_g[1][0:1, 64:64 + NE], onescol[:],
                                m_bf[:], start=True, stop=True),
             reads=[("onescol",), ("memb", par)], writes=[("ps_g", 1)])
        p.op("vector",
             lambda E, sc1=sc1, scE=scE, pr=pr, fsi=fsi, m_bf=m_bf, bsb=bsb, sf3=sf3, mwb16=mwb16: E.scalar_tensor_tensor(sf3[:], ps_g[0][:, 0:NE], 1.0,
                                              bsb[:], ALU.mult, ALU.add),
             reads=[("ps_g", 0), ("bsb", par)], writes=[("sf", par)])
        p.op("vector",
             lambda E, sc1=sc1, scE=scE, pr=pr, fsi=fsi, m_bf=m_bf, bsb=bsb, sf3=sf3, mwb16=mwb16: E.tensor_tensor(sf3[:], sf3[:], boundc[:], ALU.min),
             reads=[("sf", par), ("boundc",)], writes=[("sf", par)])
        p.op("vector",
             lambda E, tt=tt, sc1=sc1, scE=scE, pr=pr, fsi=fsi, m_bf=m_bf, bsb=bsb, sf3=sf3, mwb16=mwb16: E.tensor_scalar(scE["tmp"][:], maskw[:, tt, :],
                                              0.0, None, ALU.is_le),
             reads=[("mw", tt)], writes=[("tmpE", par)])
        p.op("vector",
             lambda E, sc1=sc1, scE=scE, pr=pr, fsi=fsi, m_bf=m_bf, bsb=bsb, sf3=sf3, mwb16=mwb16: E.scalar_tensor_tensor(sf3[:], scE["tmp"][:], 1e6,
                                              sf3[:], ALU.mult, ALU.add),
             reads=[("tmpE", par), ("sf", par)], writes=[("sf", par)])
        p.op("vector",
             lambda E, sc1=sc1, scE=scE, pr=pr, fsi=fsi, m_bf=m_bf, bsb=bsb, sf3=sf3, mwb16=mwb16: E.tensor_scalar(sf3[:], sf3[:], float(NE * CAP), None,
                                       ALU.min),
             reads=[("sf", par)], writes=[("sf", par)])
        p.op("vector",
             lambda E, sc1=sc1, scE=scE, pr=pr, fsi=fsi, m_bf=m_bf, bsb=bsb, sf3=sf3, mwb16=mwb16: E.tensor_tensor(base_row[:], base_row[:],
                                       ps_g[1][0:1, 64:64 + NE], ALU.add),
             reads=[("base",), ("ps_g", 1)], writes=[("base",)])
        p.op("vector",
             lambda E, sc1=sc1, scE=scE, pr=pr, fsi=fsi, m_bf=m_bf, bsb=bsb, sf3=sf3, mwb16=mwb16: E.tensor_reduce(sc1["m1"][:], sf3[:], axis=AX.X,
                                       op=ALU.min),
             reads=[("sf", par)], writes=[("m1", par)])
        p.op("vector",
             lambda E, sc1=sc1, scE=scE, pr=pr, fsi=fsi, m_bf=m_bf, bsb=bsb, sf3=sf3, mwb16=mwb16: E.tensor_scalar(scE["eq1"][:], sf3[:], sc1["m1"][:],
                                       None, ALU.is_le),
             reads=[("sf", par), ("m1", par)], writes=[("eq1", par)])
        p.op("vector",
             lambda E, sc1=sc1, scE=scE, pr=pr, fsi=fsi, m_bf=m_bf, bsb=bsb, sf3=sf3, mwb16=mwb16: E.scalar_tensor_tensor(scE["tmp"][:], scE["eq1"][:],
                                              1e6, sf3[:], ALU.mult, ALU.add),
             reads=[("eq1", par), ("sf", par)], writes=[("tmpE", par)])
        p.op("vector",
             lambda E, sc1=sc1, scE=scE, pr=pr, fsi=fsi, m_bf=m_bf, bsb=bsb, sf3=sf3, mwb16=mwb16: E.tensor_reduce(sc1["m2"][:], scE["tmp"][:],
                                       axis=AX.X, op=ALU.min),
             reads=[("tmpE", par)], writes=[("m2", par)])
        p.op("vector",
             lambda E, sc1=sc1, scE=scE, pr=pr, fsi=fsi, m_bf=m_bf, bsb=bsb, sf3=sf3, mwb16=mwb16: E.tensor_scalar(scE["eq2"][:], scE["tmp"][:],
                                       sc1["m2"][:], None, ALU.is_le),
             reads=[("tmpE", par), ("m2", par)], writes=[("eq2", par)])
        for j, (fs, eqk) in enumerate((("m1", "eq1"), ("m2", "eq2"))):
            p.op("vector",
                 lambda E, tt=tt, eqk=eqk, sc1=sc1, scE=scE, pr=pr, fsi=fsi, m_bf=m_bf, bsb=bsb, sf3=sf3, mwb16=mwb16: E.tensor_tensor(
                     scE["lg"][:], maskw[:, tt, :], scE[eqk][:], ALU.mult),
                 reads=[("mw", tt), (eqk,)], writes=[("lg", par)])
            p.op("vector",
                 lambda E, j=j, sc1=sc1, scE=scE, pr=pr, fsi=fsi, m_bf=m_bf, bsb=bsb, sf3=sf3, mwb16=mwb16: E.reduce_sum(pr[j][:, 1:2], scE["lg"][:],
                                             axis=AX.X),
                 reads=[("lg", par)], writes=[("prw", par, j)])
            p.op("vector",
                 lambda E, j=j, tt=tt, sc1=sc1, scE=scE, pr=pr, fsi=fsi, m_bf=m_bf, bsb=bsb, sf3=sf3, mwb16=mwb16: E.tensor_copy(pr[j][:, 0:1],
                                                     iotok_sb[:, tt:tt + 1]),
                 reads=[("iotok",)], writes=[("prt", par, j)])
            p.op("vector",
                 lambda E, j=j, tt=tt, sc1=sc1, scE=scE, pr=pr, fsi=fsi, m_bf=m_bf, bsb=bsb, sf3=sf3, mwb16=mwb16: E.tensor_scalar(
                     pr[j][:, 2:3], iotok_sb[:, tt:tt + 1],
                     float(j * (T + 128)), None, ALU.add),
                 reads=[("iotok",)], writes=[("prd", par, j)])
            p.op("vector",
                 lambda E, j=j, fs=fs, sc1=sc1, scE=scE, pr=pr, fsi=fsi, m_bf=m_bf, bsb=bsb, sf3=sf3, mwb16=mwb16: E.tensor_copy(fsi[j][:],
                                                     sc1[fs][:]),
                 reads=[(fs,)], writes=[("fsi", par, j)])
            p.op("gpsimd",
                 lambda E, j=j, sc1=sc1, scE=scE, pr=pr, fsi=fsi, m_bf=m_bf, bsb=bsb, sf3=sf3, mwb16=mwb16: E.indirect_dma_start(
                     d_tbl, IndirectOffsetOnAxis(fsi[j][:], 0), pr[j][:],
                     None),
                 reads=[("prw", par, j), ("prt", par, j), ("prd", par, j), ("prpad", par, j),
                        ("fsi", par, j), ("tbl",)],
                 writes=[("tblw", tt, j)], dma_sem=s_tblw)

    if DBG:
        d_dbg_cat = nc.dram_tensor("dbg_cat", [128, 8, T], BF16,
                                   kind="ExternalOutput").ap()
        d_dbg_x1T = nc.dram_tensor("dbg_x1T", [128, 8, T], BF16,
                                   kind="ExternalOutput").ap()
        d_dbg_mw = nc.dram_tensor("dbg_mw", [128, NTT, NE], F32,
                                  kind="ExternalOutput").ap()
        s_dbg = p.sem("s_dbg")
        dma("sync", d_dbg_cat, concatT[:], s_dbg,
            reads=[("cat", c, tt) for c in range(8) for tt in range(NTT)])
        dma("sync", d_dbg_x1T, actT[:], s_dbg,
            reads=[("actT", c, tt) for c in range(8) for tt in range(NTT)])
        dma("sync", d_dbg_mw, maskw[:], s_dbg,
            reads=[("mw", tt) for tt in range(NTT)])
        p._wait("sync", s_dbg, p.cnt[s_dbg])

    NST = CAP // 128
    CH = min(1024, CAP)
    NCH = CAP // CH
    NSTC = CH // 128
    c_dead = [("woT",), ("bo",), ("xres", 0), ("xres", 1), ("gwT",),
              ("x1n", 0), ("x1n", 1), ("sq",), ("t", 0), ("t", 1),
              ("base",), ("boundc",), ("iotok",), ("b2a",), ("triu1",),
              ("onescol",), ("onesf",)] + \
             [(k, i) for i in range(2)
              for k in ("bsb", "sf", "memb", "mwb", "lg", "eq1", "eq2",
                        "tmpE")] + \
             [(k, i, j) for i in range(2) for j in range(2)
              for k in ("prpad", "prw", "prt", "prd", "fsi")] + \
             [("mwT", tt) for tt in range(NTT)] + \
             [("cat", c, tt) for c in range(8) for tt in range(NTT)] + \
             [("actT", c, tt) for c in range(8) for tt in range(NTT)]
    es_c.close()
    es_b1.close()
    es_act.close()
    w1_sb = alloc(es_prog, "w1_sb", [128, 8, FF], BF16)
    w2_sb = alloc(es_prog, "w2_sb", [128, 16, D], BF16)
    x1g = [alloc(es_prog, f"x1g{i}", [128, NSTC, D], BF16) for i in range(2)]
    x1gT = alloc(es_prog, "x1gT", [128, 8, CH], BF16)
    hh_sb = alloc(es_prog, "hh_sb", [128, 16, CH], BF16)
    yw = [alloc(es_prog, f"yw{i}", [128, D], BF16) for i in range(2)]
    tw_sb = alloc(es_prog, "tw_sb", [128, NIC, 4], F32)
    idx_i32 = alloc(es_prog, "idx_i32", [128, NIC], I32)
    idx2_i32 = alloc(es_prog, "idx2_i32", [128, NIC], I32)
    b1_sb = [alloc(es_prog, f"b1_sb{i}", [128, 16], F32) for i in range(2)]
    t2_f32 = [alloc(es_prog, f"t2_f32{i}", [128, D], F32) for i in range(2)]
    x1l = [alloc(es_prog, f"x1l{i}", [128, D], BF16) for i in range(2)]
    accl = [alloc(es_prog, f"accl{i}", [128, D], F32) for i in range(2)]
    y1l = [alloc(es_prog, f"y1l{i}", [128, D], BF16) for i in range(2)]
    y2l = [alloc(es_prog, f"y2l{i}", [128, D], BF16) for i in range(2)]
    ostg = [alloc(es_prog, f"ostg{i}", [128, D], F32) for i in range(2)]

    es_ps_3.close()
    ps_hh = [pal(es_prog, f"ps_hh{i}", [128, 512], F32) for i in range(2)]
    ps_y = ps_out
    ps_fi = [pal(es_prog, f"ps_fi{i}", [128, 128], BF16) for i in range(2)]
    p3ps = [("ps_g", 0), ("ps_g", 1), ("ps_t3", 0), ("ps_t3", 1)]
    p.barrier("tensor", p3ps)
    p.barrier("gpsimd", c_dead)
    p.barrier("sync", c_dead)
    p.barrier("vector", c_dead)
    p.barrier("scalar", c_dead)
    p.barrier("tensor", c_dead)

    p.op("vector", lambda E: E.memset(x1g[0][:, 0, :], 0.0),
         writes=[("x1g", 0, 0)])
    dma("sync", d_x1d[T:T + 128, :], x1g[0][:, 0, :], s_init2,
        reads=[("x1g", 0, 0)], writes=[("x1d", "trash")])
    dma("sync", tw_sb[:],
        d_tbl[0:NE * CAP, :].rearrange("(q r) c -> r q c", r=128),
        s_init2,
        reads=[("tblw", tt, j) for tt in range(NTT) for j in range(2)] +
              [("tbl",)],
        writes=[("tw",)])
    p.op("vector",
         lambda E: E.tensor_copy(idx_i32[:], tw_sb[:, :, 0]),
         reads=[("tw",)], writes=[("idx",)])
    p.op("vector",
         lambda E: E.tensor_copy(idx2_i32[:], tw_sb[:, :, 2]),
         reads=[("tw",)], writes=[("idx2",)])

    phh = 0
    pyy = 0
    chunks = [(e, ch) for e in range(NE) for ch in range(NCH)]

    def emit_w1(e):
        dma("gpsimd", b1_sb[e % 2][:], d_b1T[e], s_bb[e % 2],
            writes=[("b1", e % 2)])
        for half in range(2):
            dma("gpsimd", w1_sb[:, :, half * 1024:(half + 1) * 1024],
                d_w1[e].rearrange("(c q) f -> q c f", q=128)
                [:, :, half * 1024:(half + 1) * 1024],
                s_w1, writes=[("w1", half)])

    def emit_w2(e):
        for half in range(2):
            dma("gpsimd", w2_sb[:, half * 8:(half + 1) * 8, :],
                d_w2[e].rearrange("(c q) m -> q c m", q=128)
                [:, half * 8:(half + 1) * 8, :],
                s_w2, writes=[("w2", half)])

    def emit_gathers(ci):
        e, ch = chunks[ci]
        gpar = ci % 2
        for st in range(NSTC):
            icol = e * (CAP // 128) + ch * NSTC + st
            p.op("gpsimd",
                 lambda E, icol=icol, st=st, gpar=gpar: E.indirect_dma_start(
                     x1g[gpar][:, st, :], None, d_x1d,
                     IndirectOffsetOnAxis(idx_i32[:, icol:icol + 1], 0)),
                 reads=[("idx",), ("x1d", "trash")] +
                       [("x1d", tt) for tt in range(NTT)],
                 writes=[("x1g", gpar, st)], dma_sem=s_gx[gpar])

    emit_gathers(0)
    for ci, (e, ch) in enumerate(chunks):
        gpar = ci % 2
        bpar = e % 2
        if ci + 1 < len(chunks):
            emit_gathers(ci + 1)
        if ci == 0:
            emit_w1(0)
            emit_w2(0)
        for st in range(NSTC):
            for c in range(8):
                par = (st * 8 + c) % 2
                p.op("tensor",
                     lambda E, st=st, c=c, par=par, gpar=gpar: E.transpose(
                         ps_fi[par][:],
                         x1g[gpar][:, st, c * 128:(c + 1) * 128],
                         ident_bf[:]),
                     reads=[("x1g", gpar, st), ("ident",)],
                     writes=[("ps_fi", par)])
                p.op("vector",
                     lambda E, st=st, c=c, par=par: E.tensor_copy(
                         x1gT[:, c, st * 128:(st + 1) * 128],
                         ps_fi[par][:]),
                     reads=[("ps_fi", par)], writes=[("x1gT", st)])
        for f in range(16):
            for n0 in range(0, CH, 512):
                w = min(512, CH - n0)
                par = phh % 2
                phh += 1

                def mmhh(E, f=f, n0=n0, w=w, par=par):
                    for c in range(8):
                        inst = E.matmul(
                            ps_hh[par][:, 0:w],
                            w1_sb[:, c, f * 128:(f + 1) * 128],
                            x1gT[:, c, n0:n0 + w],
                            start=(c == 0), stop=(c == 7))
                    return inst
                p.op("tensor", mmhh,
                     reads=[("w1", f // 8)] +
                           [("x1gT", st) for st in range(n0 // 128,
                                                         (n0 + w) // 128)],
                     writes=[("ps_hh", par)])
                p.op("scalar",
                     lambda E, f=f, n0=n0, w=w, par=par, bpar=bpar:
                     E.activation(hh_sb[:, f, n0:n0 + w],
                                  ps_hh[par][:, 0:w], AF.Gelu,
                                  bias=b1_sb[bpar][:, f:f + 1]),
                     reads=[("ps_hh", par), ("b1", bpar)],
                     writes=[("hh", f)])
        if ch == NCH - 1 and e + 1 < NE:
            emit_w1(e + 1)
        for st in range(NSTC):
            icol = e * (CAP // 128) + ch * NSTC + st
            ypar = pyy % 2
            pyy += 1

            def mmy(E, st=st, ypar=ypar):
                for f in range(16):
                    for n in range(2):
                        inst = E.matmul(
                            ps_y[ypar][:, n * 512:(n + 1) * 512],
                            hh_sb[:, f, st * 128:(st + 1) * 128],
                            w2_sb[:, f, n * 512:(n + 1) * 512],
                            start=(f == 0), stop=(f == 15))
                return inst
            p.op("tensor", mmy,
                 reads=[("hh", f) for f in range(16)] +
                       [("w2", 0), ("w2", 1)],
                 writes=[("ps_out", ypar)])
            p.op("vector",
                 lambda E, icol=icol, ypar=ypar: E.tensor_scalar(
                     yw[ypar][:], ps_y[ypar][:],
                     tw_sb[:, icol, 1:2], None, ALU.mult),
                 reads=[("ps_out", ypar), ("tw",)],
                 writes=[("yw", ypar)])
            p.op("gpsimd",
                 lambda E, icol=icol, ypar=ypar:
                 E.indirect_dma_start(
                     d_accd, IndirectOffsetOnAxis(
                         idx2_i32[:, icol:icol + 1], 0),
                     yw[ypar][:], None),
                 reads=[("yw", ypar), ("idx2",)],
                 writes=[("accw", ci, st)], dma_sem=s_acc[0])
        if ch == NCH - 1 and e + 1 < NE:
            emit_w2(e + 1)

    acc_done = [("accw", ci, st) for ci in range(len(chunks))
                for st in range(NSTC)]
    for tt in range(NTT):
        par = tt % 2
        dma("sync", accl[par][:], d_accB[tt * 128:(tt + 1) * 128, :],
            s_xres[par],
            reads=[("accd_init", tt)], writes=[("accl", par)])
        dma("sync", y1l[par][:], d_accd[tt * 128:(tt + 1) * 128, :],
            s_y1l[par], reads=acc_done, writes=[("y1l", par)])
        dma("sync", y2l[par][:],
            d_accd[(T + 128) + tt * 128:(T + 128) + (tt + 1) * 128, :],
            s_y2l[par], reads=acc_done, writes=[("y2l", par)])
        dma("sync", x1l[par][:], d_x1d[tt * 128:(tt + 1) * 128, :],
            s_x1l[par], reads=[("x1d", tt)], writes=[("x1l", par)])
        p.op("vector",
             lambda E, par=par: E.tensor_tensor(
                 t2_f32[par][:], accl[par][:], y1l[par][:], ALU.add),
             reads=[("accl", par), ("y1l", par)], writes=[("t2", par)])
        p.op("vector",
             lambda E, par=par: E.tensor_tensor(
                 t2_f32[par][:], t2_f32[par][:], y2l[par][:], ALU.add),
             reads=[("t2", par), ("y2l", par)], writes=[("t2", par)])
        p.op("vector",
             lambda E, par=par: E.tensor_tensor(
                 t2_f32[par][:], t2_f32[par][:], x1l[par][:], ALU.add),
             reads=[("t2", par), ("x1l", par)], writes=[("t2", par)])
        p.op("vector",
             lambda E, par=par: E.tensor_scalar(
                 t2_f32[par][:], t2_f32[par][:], 100.0, -100.0,
                 ALU.min, ALU.max),
             reads=[("t2", par)], writes=[("t2", par)])
        p.op("vector",
             lambda E, par=par: E.scalar_tensor_tensor(
                 t2_f32[par][:], t2_f32[par][:], 1.0, x1l[par][:],
                 ALU.mult, ALU.add),
             reads=[("t2", par), ("x1l", par)], writes=[("t2", par)])
        opar = par
        ln_stats_and_apply(ostg[opar][:], ("ostg", opar), par, t2_f32[par],
                           ("t2", par))
        dma("sync", d_out[tt * 128:(tt + 1) * 128, :], ostg[opar][:],
            s_out[opar], reads=[("ostg", opar)])

    p._wait("sync", s_out[0], p.cnt[s_out[0]])
    p._wait("sync", s_out[1], p.cnt[s_out[1]])

    with nc.Block() as block:
        @block.tensor
        def _(E):
            for fn in p.ops["tensor"]:
                fn(E)

        @block.scalar
        def _(E):
            for fn in p.ops["scalar"]:
                fn(E)

        @block.vector
        def _(E):
            for fn in p.ops["vector"]:
                fn(E)

        @block.gpsimd
        def _(E):
            for fn in p.ops["gpsimd"]:
                fn(E)

        @block.sync
        def _(E):
            for fn in p.ops["sync"]:
                fn(E)

    es_prog.close()
    return nc


def make_inputs(T, xb, Wq, bq, Wk, bk, Wv, bv, Wo, bo, gw, w1, b1, w2, b2):
    f = np.float32
    return {
        "x": np.ascontiguousarray(xb, dtype=f),
        "wqT": np.ascontiguousarray(Wq.T, dtype=f),
        "wkT": np.ascontiguousarray(Wk.T, dtype=f),
        "wvT": np.ascontiguousarray(Wv.T, dtype=f),
        "woT": np.ascontiguousarray(Wo.T, dtype=f),
        "bqT": np.ascontiguousarray(np.asarray(bq, f).reshape(8, 128).T),
        "bkT": np.ascontiguousarray(np.asarray(bk, f).reshape(8, 128).T),
        "bv": np.ascontiguousarray(np.asarray(bv, f).reshape(1, D)),
        "bo": np.ascontiguousarray(np.asarray(bo, f).reshape(1, D)),
        "gwT": np.ascontiguousarray(np.asarray(gw, f).T),
        "w1": np.ascontiguousarray(w1, dtype=f),
        "b1T": np.ascontiguousarray(
            np.asarray(b1, f).reshape(NE, 16, 128).transpose(0, 2, 1)),
        "w2": np.ascontiguousarray(w2, dtype=f),
        "b2": np.ascontiguousarray(b2, dtype=f),
        "ident": np.eye(128, dtype=f),
        "tri": np.triu(np.ones((128, 128), f)),
        "onesrow": np.ones((1, 128), f),
        "triu1": np.triu(np.ones((128, 128), f), 1),
        "ebase": (np.arange(NE, dtype=f) * (T // 2)).reshape(1, NE),
        "boundv": np.tile((np.arange(NE, dtype=f) * (T // 2) +
                           (T // 2) - 1).reshape(1, NE), (128, 1)),
        "iotok": np.arange(128, dtype=f)[:, None] +
                 (np.arange(T // 128, dtype=f) * 128)[None, :],
        "tbli": np.tile(np.array([[float(T), 0.0, float(T), 0.0]], f),
                        (NE * (T // 2) + 128, 1)),
    }


def kernel(L0_attn_x, L0_attn_Wq, L0_attn_bq, L0_attn_Wk, L0_attn_bk,
           L0_attn_Wv, L0_attn_bv, L0_attn_Wo, L0_attn_bo,
           L0_ffn_gate_w, L0_ffn_w1, L0_ffn_b1, L0_ffn_w2, L0_ffn_b2):
    T = 4096
    x = np.asarray(L0_attn_x)
    nc = build_nc(T)
    common = None
    in_maps = []
    for b in range(8):
        m = make_inputs(T, x[b], L0_attn_Wq, L0_attn_bq, L0_attn_Wk,
                        L0_attn_bk, L0_attn_Wv, L0_attn_bv, L0_attn_Wo,
                        L0_attn_bo, L0_ffn_gate_w, L0_ffn_w1, L0_ffn_b1,
                        L0_ffn_w2, L0_ffn_b2)
        if common is None:
            common = m
        else:
            for k in m:
                if k != "x":
                    m[k] = common[k]
        in_maps.append(m)
    res = run_bass_kernel_spmd(nc, in_maps, core_ids=list(range(8)))
    return np.stack([res.results[b]["out"] for b in range(8)], axis=0)
